# version 06
# speedup vs baseline: 1.2760x; 1.1240x over previous
import contextlib
import math

import numpy as np
import ml_dtypes

import concourse.bass as bass
import concourse.mybir as mybir
from concourse.bass_utils import run_bass_kernel_spmd

F32 = mybir.dt.float32
BF16 = mybir.dt.bfloat16
AF = mybir.ActivationFunctionType
ALU = mybir.AluOpType
AX = mybir.AxisListType
BF = ml_dtypes.bfloat16

NCORE = 8
S_FULL = 8192
TOK = S_FULL // NCORE
D = 2048
NCH = D // 128
DH = 1024
HID = 8192
NHEAD = 16
HDIM = 128
INV2PI = float(1.0 / (2.0 * math.pi))
TWOPI = float(2.0 * math.pi)
MAGIC = 8388608.0
ISQHD = float(1.0 / math.sqrt(HDIM))

ENGINES = ("sp", "act", "dve", "pe", "pool")


class Prog:

    NSLOT = 6

    def __init__(self, nc):
        self.nc = nc
        self.stack = contextlib.ExitStack()
        self.sems = {}
        for e in ENGINES:
            self.sems[(e, "c")] = self.stack.enter_context(nc.semaphore(f"s_{e}c"))
            for i in range(self.NSLOT):
                self.sems[(e, f"d{i}")] = self.stack.enter_context(
                    nc.semaphore(f"s_{e}d{i}"))
        self.count = {k: 0 for k in self.sems}
        self.dmaidx = {e: 0 for e in ENGINES}
        self.queue = {e: [] for e in ENGINES}
        self.last_writer = {}
        self.readers = {}
        self.satisfied = {e: {k: 0 for k in self.sems} for e in ENGINES}
        self.nops = 0
        self._consts = {}

    def sb(self, name, shape, dtype, scope=None):
        return (scope or self.stack).enter_context(
            self.nc.sbuf_tensor(name, shape, dtype))

    def ps(self, name, shape, dtype=F32, scope=None):
        return (scope or self.stack).enter_context(
            self.nc.psum_tensor(name, shape, dtype))

    def collective(self, kind, ins, outs, rkeys, wkeys):
        rg = [list(range(NCORE))]
        self.op("pool", lambda E, k=kind, i=ins, o=outs, g=rg:
                E.collective_compute(k, ALU.bypass, replica_groups=g,
                                     ins=i, outs=o),
                reads=_t(rkeys), writes=_t(wkeys), dma=True)

    def dram(self, name, shape, dtype=F32):
        return self.nc.dram_tensor(name, shape, dtype, kind="Internal").ap()

    def op(self, eng, fn, reads=(), writes=(), dma=False):
        if dma:
            slot = self.dmaidx[eng] % self.NSLOT
            self.dmaidx[eng] += 1
            semkey = (eng, f"d{slot}")
            inc = 16
        else:
            semkey = (eng, "c")
            inc = 1
        waits = {}

        def merge(d):
            for sk, c in d.items():
                if c > waits.get(sk, 0):
                    waits[sk] = c

        if dma and self.count[semkey] > 0:
            waits[semkey] = self.count[semkey]
        for k in reads:
            merge(self.last_writer.get(k, {}))
        for k in writes:
            merge(self.last_writer.get(k, {}))
            merge(self.readers.get(k, {}))
        sat = self.satisfied[eng]
        need = [(sk, c) for sk, c in waits.items() if c > sat[sk]]
        for sk, c in need:
            sat[sk] = c
        sem = self.sems[semkey]
        self.count[semkey] += inc
        sems = self.sems
        self.nops += 1

        def emit(engobj, need=tuple(need), fn=fn, sem=sem, inc=inc):
            for sk, c in need:
                engobj.wait_ge(sems[sk], c)
            fn(engobj).then_inc(sem, inc)

        self.queue[eng].append(emit)
        cnt = self.count[semkey]
        for k in writes:
            w = self.last_writer.setdefault(k, {})
            if cnt > w.get(semkey, 0):
                w[semkey] = cnt
            self.readers[k] = {}
        for k in reads:
            r = self.readers.setdefault(k, {})
            if cnt > r.get(semkey, 0):
                r[semkey] = cnt

    def phase(self):
        counts = {k: c for k, c in self.count.items() if c > 0}
        sems = self.sems
        for e in ENGINES:
            sat = self.satisfied[e]
            need = [(k, c) for k, c in counts.items() if c > sat[k]]
            for k, c in need:
                sat[k] = c
            if need:
                def bar(engobj, need=tuple(need)):
                    for k, c in need:
                        engobj.wait_ge(sems[k], c)
                self.queue[e].append(bar)
        self.last_writer = {}
        self.readers = {}

    def finish(self):
        counts = dict(self.count)
        sems = self.sems
        sat = self.satisfied["sp"]

        def fin(engobj):
            for key, c in counts.items():
                if c > 0 and c > sat.get(key, 0):
                    engobj.wait_ge(sems[key], c)

        self.queue["sp"].append(fin)
        secs = {"sp": "sync", "act": "scalar", "dve": "vector", "pe": "tensor",
                "pool": "gpsimd"}
        with self.nc.Block() as block:
            for e in ENGINES:
                if self.queue[e]:
                    getattr(block, secs[e])(
                        lambda engobj, q=tuple(self.queue[e]): [f(engobj) for f in q])

    def dma(self, eng, dst, src, rkey, wkey):
        self.op(eng, lambda E, d=dst, s=src: E.dma_start(out=d, in_=s),
                reads=_t(rkey), writes=_t(wkey), dma=True)

    def mm(self, out, lhsT, rhs, reads, writes, start=True, stop=True):
        self.op("pe", lambda E, o=out, l=lhsT, r=rhs, a=start, b=stop:
                E.matmul(o, l, r, start=a, stop=b),
                reads=_t(reads), writes=_t(writes))

    def cbias(self, value):
        if value not in self._consts:
            t = self.sb(f"cconst{len(self._consts)}", [128, 1], F32)
            self.memset(t[:, :], float(value), writes="consts")
            self._consts[value] = t
        return self._consts[value]

    def act(self, out, in_, func, reads, writes, bias=0.0, scale=1.0):
        reads = _t(reads)
        if isinstance(bias, float) and bias != 0.0:
            nparts = out.shape[0]
            bias = self.cbias(bias)[0:nparts, 0:1]
            reads = reads + ("consts",)
        self.op("act", lambda E, o=out, i=in_, f=func, b=bias, s=scale:
                E.activation(out=o, in_=i, func=f, bias=b, scale=s),
                reads=reads, writes=_t(writes))

    def tt(self, out, a, b, opk, reads, writes, eng="dve"):
        self.op(eng, lambda E, o=out, x=a, y=b, k=opk: E.tensor_tensor(o, x, y, k),
                reads=_t(reads), writes=_t(writes))

    def ts(self, out, in_, s1, s2, op0, op1, reads, writes, eng="dve"):
        self.op(eng, lambda E, o=out, i=in_, a=s1, b=s2, k0=op0, k1=op1:
                E.tensor_scalar(o, i, a, b, k0, k1),
                reads=_t(reads), writes=_t(writes))

    def stt(self, out, in0, scalar, in1, op0, op1, reads, writes, eng="dve"):
        self.op(eng, lambda E, o=out, i=in0, s=scalar, j=in1, k0=op0, k1=op1:
                E.scalar_tensor_tensor(o, i, s, j, k0, k1),
                reads=_t(reads), writes=_t(writes))

    def copy(self, out, in_, reads, writes, eng="dve"):
        self.op(eng, lambda E, o=out, i=in_: E.tensor_copy(o, i),
                reads=_t(reads), writes=_t(writes))

    def scan(self, out, d0, d1, init, reads, writes):
        self.op("dve", lambda E, o=out, a=d0, b=d1, i=init:
                E.tensor_tensor_scan(o, a, b, i, ALU.mult, ALU.add),
                reads=_t(reads), writes=_t(writes))

    def memset(self, ap, val, writes, eng="pool"):
        self.op(eng, lambda E, a=ap, v=val: E.memset(a, v), writes=_t(writes))

    def reduce(self, out, in_, opk, reads, writes, negate=False, eng="dve"):
        self.op(eng, lambda E, o=out, i=in_, k=opk, n=negate:
                E.tensor_reduce(o, i, AX.X, k, negate=n),
                reads=_t(reads), writes=_t(writes))


def _t(x):
    if isinstance(x, str):
        return (x,)
    return tuple(x)



def row_to_cols(p, row_ap, rkey, col_tile, ckey, pst, nchunk, ones, scale=1.0,
                off=0):
    for m in range(nchunk):
        p.mm(pst[:, m:m + 1], row_ap[0:1, m * 128:(m + 1) * 128], ones[0:1, 0:1],
             reads=(rkey, "ones"), writes=("rtc_ps",))
    p.act(col_tile[:, off:off + nchunk], pst[:, 0:nchunk], AF.Copy,
          reads=("rtc_ps",), writes=_t(ckey), scale=scale)


def emit_ln_affine(p, x, xkey, h, hkey, mrow, mkey, aoff, ones, tag, tok,
                  psA, psB, psC, fscr, onesrow=None):
    fk = f"lnf{tag}"
    sq = fscr[:, 0:512]
    tmp = fscr[:, 0:512]
    srow = fscr[0:1, 512:512 + 2 * tok]
    stat = fscr[0:1, 512 + 2 * tok:512 + 4 * tok]
    carow = fscr[0:1, 512 + 4 * tok:512 + 4 * tok + D]
    var = srow[0:1, tok:2 * tok]
    sa = mrow[0:1, aoff:aoff + D]
    p.ts(carow, mrow[0:1, aoff + D:aoff + 2 * D], 1.0, None, ALU.add,
         ALU.bypass, reads=mkey, writes=fk)
    for n0 in range(0, tok, 512):
        for c in range(NCH):
            xc = x[:, c * tok + n0:c * tok + n0 + 512]
            p.mm(psA[0:1, :], ones[:, 0:1], xc, reads=(xkey, "ones"),
                 writes="lnpsA", start=(c == 0), stop=(c == NCH - 1))
            p.act(sq, xc, AF.Square, reads=xkey, writes=fk + "sq")
            p.mm(psB[0:1, :], ones[:, 0:1], sq, reads=(fk + "sq", "ones"),
                 writes="lnpsB", start=(c == 0), stop=(c == NCH - 1))
        p.act(srow[0:1, n0:n0 + 512], psA[0:1, :], AF.Copy, reads="lnpsA",
              writes=fk, scale=1.0 / D)
        p.act(srow[0:1, tok + n0:tok + n0 + 512], psB[0:1, :], AF.Copy,
              reads="lnpsB", writes=fk, scale=1.0 / D)
    mean = srow[0:1, 0:tok]
    msq = srow[0:1, tok:2 * tok]
    p.tt(stat[0:1, 0:tok], mean, mean, ALU.mult, reads=fk, writes=fk)
    p.tt(var, msq, stat[0:1, 0:tok], ALU.subtract, reads=(fk,), writes=fk)
    p.act(var, var, AF.Ln, reads=fk, writes=fk, bias=1e-5)
    p.act(stat[0:1, 0:tok], var, AF.Exp, reads=fk, writes=fk, scale=-0.5)
    p.tt(stat[0:1, tok:2 * tok], mean, stat[0:1, 0:tok], ALU.mult,
         reads=fk, writes=fk)
    p.ts(stat[0:1, tok:2 * tok], stat[0:1, tok:2 * tok], -1.0, None, ALU.mult,
         ALU.bypass, reads=fk, writes=fk)
    for c in range(NCH):
        for n0 in range(0, tok, 512):
            p.mm(psC[:, :], carow[0:1, c * 128:(c + 1) * 128],
                 stat[0:1, n0:n0 + 512], reads=(fk,), writes="lnpsC")
            xc = x[:, c * tok + n0:c * tok + n0 + 512]
            hc = h[:, c * tok + n0:c * tok + n0 + 512]
            p.tt(tmp, xc, psC[:, :], ALU.mult, reads=(xkey, "lnpsC"),
                 writes=fk + "t")
            p.mm(psC[:, :], carow[0:1, c * 128:(c + 1) * 128],
                 stat[0:1, tok + n0:tok + n0 + 512], reads=(fk,),
                 writes="lnpsC", start=True, stop=False)
            p.mm(psC[:, :], sa[0:1, c * 128:(c + 1) * 128],
                 onesrow[0:1, 0:512], reads=(mkey, "onesrow"),
                 writes="lnpsC", start=False, stop=True)
            p.tt(hc, tmp, psC[:, :], ALU.add, reads=(fk + "t", "lnpsC"),
                 writes=hkey)



def build_mod(cols):
    nc = bass.Bass("TRN2", target_bir_lowering=False)
    c0 = nc.dram_tensor("c0", [1, D], F32, kind="ExternalInput").ap()
    c1 = nc.dram_tensor("c1", [1, D], F32, kind="ExternalInput").ap()
    w0 = nc.dram_tensor("w0", [D, cols], F32, kind="ExternalInput").ap()
    w1 = nc.dram_tensor("w1", [D, cols], F32, kind="ExternalInput").ap()
    o0 = nc.dram_tensor("o0", [1, cols], F32, kind="ExternalOutput").ap()
    o1 = nc.dram_tensor("o1", [1, cols], F32, kind="ExternalOutput").ap()
    p = Prog(nc)
    ct = p.sb("ct", [1, 2 * D], F32)
    st = p.sb("st", [1, 2 * D], F32)
    wt = p.sb("wt", [128, 2 * cols], F32)
    ot = p.sb("ot", [1, 2 * cols], F32)
    ones = p.sb("ones", [128, 1], F32)
    scols = p.sb("scols", [128, 32], F32)
    pst = p.ps("pst", [1, 512], F32)
    rtc_ps = p.ps("rtc_ps", [128, 16], F32)
    p.memset(ones[:, :], 1.0, writes="ones")
    p.dma("sp", ct[0:1, 0:D], c0, "c0", "ct")
    p.dma("sp", ct[0:1, D:2 * D], c1, "c1", "ct")
    p.act(st[0:1, :], ct[0:1, :], AF.Silu, reads="ct", writes="st")
    row_to_cols(p, st[0:1, 0:D], "st", scols, "scols", rtc_ps, 16, ones)
    row_to_cols(p, st[0:1, D:2 * D], "st", scols, "scols", rtc_ps, 16, ones,
                off=16)
    for li, w in ((0, w0), (1, w1)):
        co = li * cols
        for n0 in range(0, cols, 512):
            n1 = min(n0 + 512, cols)
            for k in range(NCH):
                wkey = f"wt{k % 2}"
                wb = wt[:, co + (k % 2) * 512:co + (k % 2) * 512 + (n1 - n0)]
                p.dma("sp", wb, w[k * 128:(k + 1) * 128, n0:n1], f"w{li}", wkey)
                p.mm(pst[0:1, 0:n1 - n0],
                     scols[:, li * 16 + k:li * 16 + k + 1],
                     wb, reads=("scols", wkey), writes="pst",
                     start=(k == 0), stop=(k == NCH - 1))
            p.act(ot[0:1, co + n0:co + n1], pst[0:1, 0:n1 - n0], AF.Copy,
                  reads="pst", writes="ot")
    p.dma("sp", o0, ot[0:1, 0:cols], "ot", "o0")
    p.dma("sp", o1, ot[0:1, cols:2 * cols], "ot", "o1")
    p.finish()
    return nc



def build_l0a():
    nc = bass.Bass("TRN2", target_bir_lowering=False)
    x_d = nc.dram_tensor("x", [D, TOK], F32, kind="ExternalInput").ap()
    mod_d = nc.dram_tensor("mod0", [1, 2 * D], F32, kind="ExternalInput").ap()
    win_d = nc.dram_tensor("win", [D, D], F32, kind="ExternalInput").ap()
    bin_d = nc.dram_tensor("binr", [1, D], F32, kind="ExternalInput").ap()
    wgx_d = nc.dram_tensor("wgx", [D, 128], F32, kind="ExternalInput").ap()
    wga_d = nc.dram_tensor("wga", [D, 128], F32, kind="ExternalInput").ap()
    bgx_d = nc.dram_tensor("bgx", [1, DH], F32, kind="ExternalInput").ap()
    bga_d = nc.dram_tensor("bga", [1, DH], F32, kind="ExternalInput").ap()
    ar_d = nc.dram_tensor("ar", [1, DH], F32, kind="ExternalInput").ap()
    ai_d = nc.dram_tensor("ai", [1, DH], F32, kind="ExternalInput").ap()
    m_o = nc.dram_tensor("m_o", [DH, TOK], F32, kind="ExternalOutput").ap()
    th_o = nc.dram_tensor("th_o", [DH, TOK], F32, kind="ExternalOutput").ap()
    xr_o = nc.dram_tensor("xr_o", [DH, TOK], F32, kind="ExternalOutput").ap()
    xi_o = nc.dram_tensor("xi_o", [DH, TOK], F32, kind="ExternalOutput").ap()

    p = Prog(nc)
    ones = p.sb("ones", [128, 1], F32)
    p.memset(ones[:, :], 1.0, writes="ones")
    mrow = p.sb("mrow", [1, 2 * D], F32)
    p.dma("sp", mrow[0:1, :], mod_d, "mod_d", "mrow")

    xa = p.sb("xa", [128, 2 * TOK], F32)
    h1 = p.sb("h1", [128, NCH * TOK], BF16)
    psA = p.ps("psA", [1, 512], F32)
    psB = p.ps("psB", [1, 512], F32)
    psC = p.ps("psC", [128, 512], F32)
    sq = p.sb("sqa", [128, 512], F32)
    srow = p.sb("srowa", [1, 2 * TOK], F32)
    stat = p.sb("stata", [1, 3 * TOK], F32)
    carow = p.sb("carowa", [1, D], F32)
    var = p.sb("vara", [1, TOK], F32)
    tmp = p.sb("lntmpa", [128, 512], F32)
    p.ts(carow[0:1, :], mrow[0:1, D:2 * D], 1.0, None, ALU.add, ALU.bypass,
         reads="mrow", writes="carow")
    p.memset(stat[0:1, 2 * TOK:3 * TOK], 1.0, writes="stat")
    for n0 in range(0, TOK, 512):
        for c in range(NCH):
            xb = xa[:, (c % 2) * TOK:(c % 2) * TOK + 512]
            xkey = f"xa{c % 2}"
            p.dma("sp", xb, x_d[c * 128:(c + 1) * 128, n0:n0 + 512], "x_d", xkey)
            p.mm(psA[0:1, :], ones[:, 0:1], xb, reads=(xkey, "ones"),
                 writes="psA", start=(c == 0), stop=(c == NCH - 1))
            p.act(sq[:, :], xb, AF.Square, reads=xkey, writes="sqa")
            p.mm(psB[0:1, :], ones[:, 0:1], sq[:, :], reads=("sqa", "ones"),
                 writes="psB", start=(c == 0), stop=(c == NCH - 1))
        p.act(srow[0:1, n0:n0 + 512], psA[0:1, :], AF.Copy, reads="psA",
              writes="srow", scale=1.0 / D)
        p.act(srow[0:1, TOK + n0:TOK + n0 + 512], psB[0:1, :], AF.Copy,
              reads="psB", writes="srow", scale=1.0 / D)
    mean = srow[0:1, 0:TOK]
    msq = srow[0:1, TOK:2 * TOK]
    p.tt(var[0:1, :], mean, mean, ALU.mult, reads="srow", writes="var")
    p.tt(var[0:1, :], msq, var[0:1, :], ALU.subtract, reads=("srow", "var"),
         writes="var")
    p.act(var[0:1, :], var[0:1, :], AF.Ln, reads="var", writes="var", bias=1e-5)
    p.act(stat[0:1, 0:TOK], var[0:1, :], AF.Exp, reads="var", writes="stat",
          scale=-0.5)
    p.tt(stat[0:1, TOK:2 * TOK], mean, stat[0:1, 0:TOK], ALU.mult,
         reads=("srow", "stat"), writes="stat")
    p.ts(stat[0:1, TOK:2 * TOK], stat[0:1, TOK:2 * TOK], -1.0, None, ALU.mult,
         ALU.bypass, reads="stat", writes="stat")
    for c in range(NCH):
        xb = xa[:, (c % 2) * TOK:(c % 2) * TOK + TOK]
        xkey = f"xa{c % 2}"
        p.dma("sp", xb, x_d[c * 128:(c + 1) * 128, :], "x_d", xkey)
        for n0 in range(0, TOK, 512):
            p.mm(psC[:, :], carow[0:1, c * 128:(c + 1) * 128],
                 stat[0:1, n0:n0 + 512], reads=("carow", "stat"), writes="psC")
            p.tt(tmp[:, :], xb[:, n0:n0 + 512], psC[:, :], ALU.mult,
                 reads=(xkey, "psC"), writes="lntmpa")
            p.mm(psC[:, :], carow[0:1, c * 128:(c + 1) * 128],
                 stat[0:1, TOK + n0:TOK + n0 + 512], reads=("carow", "stat"),
                 writes="psC", start=True, stop=False)
            p.mm(psC[:, :], mrow[0:1, c * 128:(c + 1) * 128],
                 stat[0:1, 2 * TOK + n0:2 * TOK + n0 + 512],
                 reads=("mrow", "stat"), writes="psC", start=False, stop=True)
            p.tt(h1[:, c * TOK + n0:c * TOK + n0 + 512], tmp[:, :], psC[:, :],
                 ALU.add, reads=("lntmpa", "psC"), writes="h1")

    winr = win_d.rearrange("(k p) m -> p k m", p=128)
    wbuf = p.sb("wbuf", [128, 2 * D], BF16)
    bcols = p.sb("bcols", [128, 16 + 8 + 8 + 8 + 8], F32)
    rtc_ps = p.ps("rtc_ps", [128, 16], F32)
    brow = p.sb("brow", [1, D], F32)
    onesrow = p.sb("onesrow", [1, 512], F32)
    p.memset(onesrow[0:1, :], 1.0, writes="onesrow")
    p.dma("sp", brow[0:1, :], bin_d, "bin_d", "binrow")
    xh = p.sb("xh", [128, NCH * TOK], BF16)
    psX = p.ps("psX", [128, 1024], F32)
    for m in range(NCH):
        wb = wbuf[:, (m % 2) * D:(m % 2) * D + D]
        wkey = f"wbuf{m % 2}"
        for kq in range(4):
            p.dma("pool",
                  wb.rearrange("p (k m) -> p k m", k=NCH)[:, 4 * kq:4 * kq + 4, :],
                  winr[:, 4 * kq:4 * kq + 4, m * 128:(m + 1) * 128],
                  "win_d", wkey)
        for n0 in (0, 512):
            p.mm(psX[:, n0:n0 + 512], brow[0:1, m * 128:(m + 1) * 128],
                 onesrow[0:1, :], reads=("binrow", "onesrow"), writes="psX",
                 start=True, stop=False)
            for k in range(NCH):
                p.mm(psX[:, n0:n0 + 512], wb[:, k * 128:(k + 1) * 128],
                     h1[:, k * TOK + n0:k * TOK + n0 + 512],
                     reads=(wkey, "h1"), writes="psX",
                     start=False, stop=(k == NCH - 1))
        p.act(xh[:, m * TOK:(m + 1) * TOK], psX[:, :], AF.Copy,
              reads="psX", writes="xh")

    rows2 = p.sb("rows2", [1, 4 * DH], F32)
    p.dma("sp", rows2[0:1, 0:DH], bgx_d, "bgx_d", "rows2")
    p.dma("sp", rows2[0:1, DH:2 * DH], bga_d, "bga_d", "rows2")
    p.dma("sp", rows2[0:1, 3 * DH:4 * DH], ai_d, "ai_d", "rows2")
    spr = p.sb("spr", [1, 3 * DH], F32)
    p.dma("sp", spr[0:1, 0:DH], ar_d, "ar_d", "spr")
    p.act(spr[0:1, DH:2 * DH], spr[0:1, 0:DH], AF.Abs, reads="spr", writes="spr")
    p.act(spr[0:1, DH:2 * DH], spr[0:1, DH:2 * DH], AF.Exp, reads="spr",
          writes="spr", scale=-1.0)
    p.act(spr[0:1, DH:2 * DH], spr[0:1, DH:2 * DH], AF.Ln, reads="spr",
          writes="spr", bias=1.0)
    p.act(spr[0:1, 2 * DH:3 * DH], spr[0:1, 0:DH], AF.Relu, reads="spr",
          writes="spr")
    p.tt(rows2[0:1, 2 * DH:3 * DH], spr[0:1, DH:2 * DH],
         spr[0:1, 2 * DH:3 * DH], ALU.add, reads="spr", writes="rows2")
    row_to_cols(p, rows2[0:1, 0:DH], "rows2", bcols, "bcols", rtc_ps, 8, ones,
                off=16)
    row_to_cols(p, rows2[0:1, DH:2 * DH], "rows2", bcols, "bcols", rtc_ps, 8,
                ones, off=24)
    row_to_cols(p, rows2[0:1, 2 * DH:3 * DH], "rows2", bcols, "bcols", rtc_ps,
                8, ones, off=32, scale=-8.0)
    row_to_cols(p, rows2[0:1, 3 * DH:4 * DH], "rows2", bcols, "bcols", rtc_ps,
                8, ones, off=40)

    wg2 = p.sb("wg2", [128, 2 * NCH * 128], BF16)
    for k in range(NCH):
        p.dma("pool", wg2[:, k * 128:(k + 1) * 128],
              wgx_d[k * 128:(k + 1) * 128, :], "wgx_d", "wg2")
        p.dma("pool", wg2[:, (NCH + k) * 128:(NCH + k + 1) * 128],
              wga_d[k * 128:(k + 1) * 128, :], "wga_d", "wg2")

    psG = p.ps("psG", [128, 1024], F32)
    gx = p.sb("gx", [128, TOK], F32)
    ga = p.sb("ga", [128, TOK], F32)
    la = p.sb("la", [128, TOK], F32)
    sc1 = p.sb("sc1", [128, TOK], F32)
    sc2 = p.sb("sc2", [128, TOK], F32)
    for b in range(8):
        for gi, (gt, gkey, boff) in enumerate(((gx, "gx", 16), (ga, "ga", 24))):
            woff = gi * NCH * 128
            for n0 in (0, 512):
                for kk in (2 * b, 2 * b + 1):
                    p.mm(psG[:, n0:n0 + 512],
                         wg2[:, woff + kk * 128:woff + (kk + 1) * 128],
                         xh[:, kk * TOK + n0:kk * TOK + n0 + 512],
                         reads=("wg2", "xh"), writes="psG",
                         start=(kk == 2 * b), stop=(kk == 2 * b + 1))
            p.act(gt[:, :], psG[:, :], AF.Sigmoid, reads=("psG", "bcols"),
                  writes=gkey, bias=bcols[:, boff + b:boff + b + 1])
        p.ts(la[:, :], ga[:, :], bcols[:, 32 + b:33 + b], None, ALU.mult,
             ALU.bypass, reads=("ga", "bcols"), writes="la")
        p.ts(sc1[:, :], ga[:, :], bcols[:, 40 + b:41 + b], None, ALU.mult,
             ALU.bypass, reads=("ga", "bcols"), writes="sc1")
        p.dma("sp", th_o[b * 128:(b + 1) * 128, :], sc1[:, :], "sc1", "th_o")
        p.act(sc2[:, :], la[:, :], AF.Exp, reads="la", writes="sc2")
        p.dma("sp", m_o[b * 128:(b + 1) * 128, :], sc2[:, :], "sc2", "m_o")
        p.act(sc2[:, :], la[:, :], AF.Exp, reads="la", writes="sc2", scale=2.0)
        p.act(sc2[:, :], sc2[:, :], AF.Ln, reads="sc2", writes="sc2",
              scale=-1.0, bias=1.0)
        p.act(sc2[:, :], sc2[:, :], AF.Exp, reads="sc2", writes="sc2",
              scale=0.5)
        p.tt(gx[:, :], gx[:, :], sc2[:, :], ALU.mult, reads=("gx", "sc2"),
             writes="gx")
        p.tt(sc1[:, :], gx[:, :], xh[:, b * TOK:(b + 1) * TOK], ALU.mult,
             reads=("gx", "xh"), writes="sc1")
        p.dma("sp", xr_o[b * 128:(b + 1) * 128, :], sc1[:, :], "sc1", "xr_o")
        p.tt(sc2[:, :], gx[:, :], xh[:, (8 + b) * TOK:(9 + b) * TOK], ALU.mult,
             reads=("gx", "xh"), writes="sc2")
        p.dma("sp", xi_o[b * 128:(b + 1) * 128, :], sc2[:, :], "sc2", "xi_o")
    p.finish()
    return nc



def build_scan():
    nc = bass.Bass("TRN2", target_bir_lowering=False)
    m_d = nc.dram_tensor("m_i", [128, S_FULL], F32, kind="ExternalInput").ap()
    th_d = nc.dram_tensor("th_i", [128, S_FULL], F32, kind="ExternalInput").ap()
    xr_d = nc.dram_tensor("xr_i", [128, S_FULL], F32, kind="ExternalInput").ap()
    xi_d = nc.dram_tensor("xi_i", [128, S_FULL], F32, kind="ExternalInput").ap()
    hr_o = nc.dram_tensor("hr_o", [128, S_FULL], BF16, kind="ExternalOutput").ap()
    hi_o = nc.dram_tensor("hi_o", [128, S_FULL], BF16, kind="ExternalOutput").ap()
    p = Prog(nc)
    SL = 2048
    mt = p.sb("mt", [128, SL], F32)
    tht = p.sb("tht", [128, SL], F32)
    xrt = p.sb("xrt", [128, SL], F32)
    xit = p.sb("xit", [128, SL], F32)
    sn = p.sb("sn", [128, SL], F32)
    cs = p.sb("cs", [128, SL], F32)
    rA = p.sb("rA", [128, SL], F32)
    t1 = p.sb("t1", [128, SL], F32)
    t2 = p.sb("t2", [128, SL], F32)
    ur = p.sb("ur", [128, SL], F32)
    ui = p.sb("ui", [128, SL], F32)
    hb = p.sb("hb", [128, 2 * SL], BF16)
    onescol = p.sb("onescol", [128, SL], F32)
    zc = p.sb("zc", [128, 3], F32)
    p.memset(onescol[:, :], 1.0, writes="ones")
    p.memset(zc[:, :], 0.0, writes="zc")
    for s in range(S_FULL // SL):
        o = s * SL
        p.dma("sp", mt[:, :], m_d[:, o:o + SL], "m_d", "mt")
        p.dma("sp", tht[:, :], th_d[:, o:o + SL], "th_d", "tht")
        p.dma("sp", xrt[:, :], xr_d[:, o:o + SL], "xr_d", "xrt")
        p.dma("sp", xit[:, :], xi_d[:, o:o + SL], "xi_d", "xit")
        p.scan(tht[:, :], onescol[:, :], tht[:, :], zc[:, 0:1],
               reads=("ones", "tht", "zc"), writes="tht")
        p.copy(zc[:, 0:1], tht[:, SL - 1:SL], reads="tht", writes="zc")
        p.ts(rA[:, :], tht[:, :], INV2PI, None, ALU.mult, ALU.bypass,
             reads="tht", writes="rA")
        p.ts(t1[:, :], rA[:, :], MAGIC, MAGIC, ALU.add, ALU.subtract,
             reads="rA", writes="t1")
        p.tt(rA[:, :], rA[:, :], t1[:, :], ALU.subtract, reads=("rA", "t1"),
             writes="rA")
        p.act(sn[:, :], rA[:, :], AF.Sin, reads="rA", writes="sn", scale=TWOPI)
        p.ts(rA[:, :], tht[:, :], INV2PI, 0.25, ALU.mult, ALU.add,
             reads="tht", writes="rA")
        p.ts(t1[:, :], rA[:, :], MAGIC, MAGIC, ALU.add, ALU.subtract,
             reads="rA", writes="t1")
        p.tt(rA[:, :], rA[:, :], t1[:, :], ALU.subtract, reads=("rA", "t1"),
             writes="rA")
        p.act(cs[:, :], rA[:, :], AF.Sin, reads="rA", writes="cs", scale=TWOPI)
        p.tt(t1[:, :], sn[:, :], xit[:, :], ALU.mult, reads=("sn", "xit"),
             writes="t1")
        p.tt(t2[:, :], sn[:, :], xrt[:, :], ALU.mult, reads=("sn", "xrt"),
             writes="t2")
        p.tt(xrt[:, :], cs[:, :], xrt[:, :], ALU.mult, reads=("cs", "xrt"),
             writes="xrt")
        p.tt(xit[:, :], cs[:, :], xit[:, :], ALU.mult, reads=("cs", "xit"),
             writes="xit")
        p.tt(xrt[:, :], xrt[:, :], t1[:, :], ALU.add, reads=("xrt", "t1"),
             writes="xrt")
        p.tt(xit[:, :], xit[:, :], t2[:, :], ALU.subtract, reads=("xit", "t2"),
             writes="xit")
        p.scan(ur[:, :], mt[:, :], xrt[:, :], zc[:, 1:2],
               reads=("mt", "xrt", "zc"), writes="ur")
        p.copy(zc[:, 1:2], ur[:, SL - 1:SL], reads="ur", writes="zc")
        p.scan(ui[:, :], mt[:, :], xit[:, :], zc[:, 2:3],
               reads=("mt", "xit", "zc"), writes="ui")
        p.copy(zc[:, 2:3], ui[:, SL - 1:SL], reads="ui", writes="zc")
        p.tt(t1[:, :], cs[:, :], ur[:, :], ALU.mult, reads=("cs", "ur"),
             writes="t1")
        p.tt(t2[:, :], sn[:, :], ui[:, :], ALU.mult, reads=("sn", "ui"),
             writes="t2")
        p.tt(hb[:, 0:SL], t1[:, :], t2[:, :], ALU.subtract, reads=("t1", "t2"),
             writes="hb")
        p.tt(t1[:, :], sn[:, :], ur[:, :], ALU.mult, reads=("sn", "ur"),
             writes="t1")
        p.tt(t2[:, :], cs[:, :], ui[:, :], ALU.mult, reads=("cs", "ui"),
             writes="t2")
        p.tt(hb[:, SL:2 * SL], t1[:, :], t2[:, :], ALU.add, reads=("t1", "t2"),
             writes="hb")
        p.dma("sp", hr_o[:, o:o + SL], hb[:, 0:SL], "hb", "hr_o")
        p.dma("sp", hi_o[:, o:o + SL], hb[:, SL:2 * SL], "hb", "hi_o")
    p.finish()
    return nc



def build_l0b():
    nc = bass.Bass("TRN2", target_bir_lowering=False)
    x_d = nc.dram_tensor("x", [D, TOK], F32, kind="ExternalInput").ap()
    h_d = nc.dram_tensor("h", [D, TOK], BF16, kind="ExternalInput").ap()
    wout_d = nc.dram_tensor("wout", [D, D], F32, kind="ExternalInput").ap()
    bout_d = nc.dram_tensor("boutr", [1, D], F32, kind="ExternalInput").ap()
    modr_d = nc.dram_tensor("modr", [1, 4 * D], F32, kind="ExternalInput").ap()
    wg_d = nc.dram_tensor("wg", [D, HID], F32, kind="ExternalInput").ap()
    wu_d = nc.dram_tensor("wu", [D, HID], F32, kind="ExternalInput").ap()
    wd_d = nc.dram_tensor("wd", [HID, D], F32, kind="ExternalInput").ap()
    x2_d = nc.dram_tensor("x2", [D, TOK], F32, kind="ExternalOutput").ap()
    hh_d = nc.dram_tensor("hh_s", [TOK, HID], F32, kind="Internal").ap()
    g_d = nc.dram_tensor("g_s", [TOK, HID], BF16, kind="Internal").ap()

    p = Prog(nc)
    ones = p.sb("ones", [128, 1], F32)
    onesrow = p.sb("onesrow", [1, 512], F32)
    p.memset(ones[:, :], 1.0, writes="ones")
    p.memset(onesrow[0:1, :], 1.0, writes="onesrow")
    xT = p.sb("xT", [128, NCH * TOK], F32)
    modsc = p.sb("modsc", [1, 2 * D], F32)
    bcols = p.sb("bcols", [128, 32], F32)
    aF = p.sb("aF", [128, 10240], F32)
    aB = p.sb("aB", [128, 40960], BF16)
    psA = p.ps("psA", [1, 512], F32)
    psB = p.ps("psB", [1, 512], F32)
    psC = p.ps("psC", [128, 512], F32)
    rtc = p.ps("rtc", [128, 16], F32)
    psW = p.ps("psW", [128, 1024], F32)
    psG = p.ps("psG", [128, 512], F32)
    psU = p.ps("psU", [128, 512], F32)

    grow = aF[0:1, 0:2 * D]
    t1r = aF[0:1, 2 * D:3 * D]
    t2r = aF[0:1, 3 * D:4 * D]
    p.dma("sp", grow[0:1, 0:D], modr_d[0:1, 0:D], "modr", "grow")
    p.dma("sp", grow[0:1, D:2 * D], modr_d[0:1, 3 * D:4 * D], "modr", "grow")
    p.dma("sp", modsc[0:1, :], modr_d[0:1, D:3 * D], "modr", "modsc")
    for gi in range(2):
        gseg = grow[0:1, gi * D:(gi + 1) * D]
        p.act(t1r, gseg, AF.Sign, reads="grow", writes="t1r")
        p.act(t2r, gseg, AF.Abs, reads="grow", writes="t2r")
        p.ts(t2r, t2r, 0.25, None, ALU.max, ALU.bypass, reads="t2r",
             writes="t2r")
        p.tt(gseg, t1r, t2r, ALU.mult, reads=("t1r", "t2r"), writes="grow")
        row_to_cols(p, gseg, "grow", bcols, "bcols", rtc, 16, ones,
                    off=gi * 16)
    for c in range(NCH):
        p.dma("sp", xT[:, c * TOK:(c + 1) * TOK],
              x_d[c * 128:(c + 1) * 128, :], "x_d", "xT")
    hbuf = aB[:, 0:NCH * TOK]
    for c in range(NCH):
        p.dma("sp", hbuf[:, c * TOK:(c + 1) * TOK],
              h_d[c * 128:(c + 1) * 128, :], "h_d", "hbuf")
    brow = aF[0:1, 4 * D:5 * D]
    p.dma("sp", brow, bout_d, "bout", "brow")
    p.phase()
    woutr = wout_d.rearrange("(k p) m -> p k m", p=128)
    yt = aF[:, 4096:5120]
    for m in range(NCH):
        wb = aB[:, NCH * TOK + (m % 2) * D:NCH * TOK + (m % 2) * D + D]
        wkey = f"wb{m % 2}"
        for kq in range(4):
            p.dma("pool",
                  wb.rearrange("p (k m) -> p k m", k=NCH)[:, 4 * kq:4 * kq + 4, :],
                  woutr[:, 4 * kq:4 * kq + 4, m * 128:(m + 1) * 128],
                  "wout_d", wkey)
        for n0 in (0, 512):
            p.mm(psW[:, n0:n0 + 512], brow[0:1, m * 128:(m + 1) * 128],
                 onesrow[0:1, :], reads=("brow", "onesrow"), writes="psW",
                 start=True, stop=False)
            for k in range(NCH):
                p.mm(psW[:, n0:n0 + 512], wb[:, k * 128:(k + 1) * 128],
                     hbuf[:, k * TOK + n0:k * TOK + n0 + 512],
                     reads=(wkey, "hbuf"), writes="psW",
                     start=False, stop=(k == NCH - 1))
        p.act(yt, psW[:, :], AF.Copy, reads="psW", writes="yt")
        p.stt(xT[:, m * TOK:(m + 1) * TOK], yt, bcols[:, m:m + 1],
              xT[:, m * TOK:(m + 1) * TOK], ALU.mult, ALU.add,
              reads=("yt", "bcols", "xT"), writes="xT")
    p.phase()
    emit_ln_affine(p, xT, "xT", hbuf, "hbuf", modsc, "modsc", 0, ones,
                  "ln2", TOK, psA, psB, psC, aF)
    p.phase()
    wgr = wg_d.rearrange("(k p) h -> p k h", p=128)
    wur = wu_d.rearrange("(k p) h -> p k h", p=128)
    sg = aF[:, 0:512]
    su = aF[:, 512:1024]
    shh = aF[:, 1024:1536]
    for ht in range(16):
        wgt = aB[:, NCH * TOK:NCH * TOK + 16 * 512]
        wut = aB[:, NCH * TOK + 16 * 512:NCH * TOK + 32 * 512]
        for kq in range(4):
            p.dma("pool",
                  wgt.rearrange("p (k h) -> p k h", h=512)[:, 4 * kq:4 * kq + 4, :],
                  wgr[:, 4 * kq:4 * kq + 4, ht * 512:(ht + 1) * 512],
                  "wg_d", "wgt")
            p.dma("pool",
                  wut.rearrange("p (k h) -> p k h", h=512)[:, 4 * kq:4 * kq + 4, :],
                  wur[:, 4 * kq:4 * kq + 4, ht * 512:(ht + 1) * 512],
                  "wu_d", "wut")
        for tt in range(8):
            for k in range(NCH):
                p.mm(psG[:, :], hbuf[:, k * TOK + tt * 128:k * TOK + (tt + 1) * 128],
                     wgt[:, k * 512:(k + 1) * 512], reads=("hbuf", "wgt"),
                     writes="psG", start=(k == 0), stop=(k == NCH - 1))
            for k in range(NCH):
                p.mm(psU[:, :], hbuf[:, k * TOK + tt * 128:k * TOK + (tt + 1) * 128],
                     wut[:, k * 512:(k + 1) * 512], reads=("hbuf", "wut"),
                     writes="psU", start=(k == 0), stop=(k == NCH - 1))
            p.act(sg, psG[:, :], AF.Silu, reads="psG", writes="sg")
            p.act(su, psU[:, :], AF.Copy, reads="psU", writes="su")
            p.tt(shh, sg, su, ALU.mult, reads=("sg", "su"), writes="shh")
            p.dma("sp", hh_d[tt * 128:(tt + 1) * 128, ht * 512:(ht + 1) * 512],
                  shh, "shh", "hh_d")
    p.phase()
    SEC = 2048
    hh = aF[:, 0:SEC]
    ha = aF[:, SEC:2 * SEC]
    s1 = aF[:, 2 * SEC:2 * SEC + SEC // 4]
    s2 = aF[:, 2 * SEC + SEC // 4:2 * SEC + SEC // 2]
    tq = aB[:, 0:6 * (SEC // 4)]
    gt = aB[:, 6 * (SEC // 4):6 * (SEC // 4) + SEC]
    Q = SEC // 4
    for tt in range(8):
        for sec in range(HID // SEC):
            p.dma("sp", hh, hh_d[tt * 128:(tt + 1) * 128,
                                  sec * SEC:(sec + 1) * SEC], "hh_d", "hh")
            p.act(ha, hh, AF.Abs, reads="hh", writes="ha")
            hq = [ha.rearrange("p (g f) -> p g f", f=4)[:, :, i] for i in range(4)]
            pairs = [(0, 1), (0, 2), (0, 3), (1, 2), (1, 3), (2, 3)]
            for pi, (a, b) in enumerate(pairs):
                p.tt(tq[:, pi * Q:(pi + 1) * Q], hq[a], hq[b], ALU.is_ge,
                     reads="ha", writes="tq")
            t01, t02, t03, t12, t13, t23 = [tq[:, i * Q:(i + 1) * Q]
                                            for i in range(6)]
            hhq = [hh.rearrange("p (g f) -> p g f", f=4)[:, :, i] for i in range(4)]
            gq = [gt.rearrange("p (g f) -> p g f", f=4)[:, :, i] for i in range(4)]
            p.tt(s1, t01, t02, ALU.add, reads="tq", writes="s1")
            p.tt(s1, s1, t03, ALU.add, reads=("s1", "tq"), writes="s1")
            p.stt(gq[0], s1, 1.5, hhq[0], ALU.is_ge, ALU.mult,
                  reads=("s1", "hh"), writes="gt")
            p.tt(s2, t12, t13, ALU.add, reads="tq", writes="s2")
            p.tt(s2, s2, t01, ALU.subtract, reads=("s2", "tq"), writes="s2")
            p.stt(gq[1], s2, 0.5, hhq[1], ALU.is_ge, ALU.mult,
                  reads=("s2", "hh"), writes="gt")
            p.tt(s1, t02, t12, ALU.add, reads="tq", writes="s1")
            p.tt(s1, s1, t23, ALU.subtract, reads=("s1", "tq"), writes="s1")
            p.stt(gq[2], s1, 0.5, hhq[2], ALU.is_le, ALU.mult,
                  reads=("s1", "hh"), writes="gt")
            p.tt(s2, t03, t13, ALU.add, reads="tq", writes="s2")
            p.tt(s2, s2, t23, ALU.add, reads=("s2", "tq"), writes="s2")
            p.stt(gq[3], s2, 1.5, hhq[3], ALU.is_le, ALU.mult,
                  reads=("s2", "hh"), writes="gt")
            p.dma("sp", g_d[tt * 128:(tt + 1) * 128, sec * SEC:(sec + 1) * SEC],
                  gt, "gt", "g_d")
    p.phase()
    wdr = wd_d.rearrange("(k p) d -> p k d", p=128)
    for th in range(2):
        gch = aB[:, 0:64 * 512]
        for k in range(64):
            p.op("sp", lambda E, dst=gch[:, k * 512:(k + 1) * 512],
                 src=g_d[th * 512:(th + 1) * 512, k * 128:(k + 1) * 128]:
                 E.dma_start(out=dst, in_=src, transpose=True),
                 reads=("g_d",), writes=("gch",), dma=True)
        for d in range(NCH):
            wdb = aB[:, 64 * 512:64 * 512 + HID]
            wkey = "wdb"
            for kq in range(8):
                p.dma("pool",
                      wdb.rearrange("p (k m) -> p k m", m=128)[:, 8 * kq:8 * kq + 8, :],
                      wdr[:, 8 * kq:8 * kq + 8, d * 128:(d + 1) * 128],
                      "wd_d", wkey)
            for k in range(64):
                p.mm(psW[:, 0:512], wdb[:, k * 128:(k + 1) * 128],
                     gch[:, k * 512:(k + 1) * 512], reads=(wkey, "gch"),
                     writes="psW", start=(k == 0), stop=(k == 63))
            p.stt(xT[:, d * TOK + th * 512:d * TOK + th * 512 + 512],
                  psW[:, 0:512], bcols[:, 16 + d:17 + d],
                  xT[:, d * TOK + th * 512:d * TOK + th * 512 + 512],
                  ALU.mult, ALU.add, reads=("psW", "bcols", "xT"), writes="xT")
    for c in range(NCH):
        p.dma("sp", x2_d[c * 128:(c + 1) * 128, :],
              xT[:, c * TOK:(c + 1) * TOK], "xT", "x2")
    p.finish()
    return nc



def build_l1a():
    nc = bass.Bass("TRN2", target_bir_lowering=False)
    x_d = nc.dram_tensor("x", [D, TOK], F32, kind="ExternalInput").ap()
    mods_d = nc.dram_tensor("mods", [1, 2 * D], F32, kind="ExternalInput").ap()
    ind4_d = nc.dram_tensor("ind4", [4, 128], F32, kind="ExternalInput").ap()
    cW = {}
    sW = {}
    bW = {}
    for w in ("q", "k", "v"):
        cW[w] = nc.dram_tensor(f"c{w}", [D, D], F32, kind="ExternalInput").ap()
        sW[w] = nc.dram_tensor(f"s{w}", [64, D], F32, kind="ExternalInput").ap()
        bW[w] = nc.dram_tensor(f"b{w}", [1, D], F32, kind="ExternalInput").ap()
    q_o = nc.dram_tensor("q_o", [D, TOK], BF16, kind="ExternalOutput").ap()
    k_o = nc.dram_tensor("k_o", [D, TOK], BF16, kind="ExternalOutput").ap()
    v_o = nc.dram_tensor("v_o", [TOK, D], BF16, kind="ExternalOutput").ap()
    qn_o = nc.dram_tensor("qn_o", [1, NHEAD], F32, kind="ExternalOutput").ap()
    kn_o = nc.dram_tensor("kn_o", [1, NHEAD], F32, kind="ExternalOutput").ap()

    p = Prog(nc)
    ones = p.sb("ones", [128, 1], F32)
    onesrow = p.sb("onesrow", [1, 512], F32)
    p.memset(ones[:, :], 1.0, writes="ones")
    p.memset(onesrow[0:1, :], 1.0, writes="onesrow")
    xT = p.sb("xT", [128, NCH * TOK], F32)
    modsc = p.sb("modsc", [1, 2 * D], F32)
    ind4 = p.sb("ind4t", [4, 128], F32)
    sc4 = p.sb("sc4", [4, NCH * 128], F32)
    nrow = p.sb("nrow", [1, 2 * NHEAD], F32)
    aF = p.sb("aF", [128, 11264], F32)
    aB = p.sb("aB", [128, 28672], BF16)
    psA = p.ps("psA", [1, 512], F32)
    psB = p.ps("psB", [1, 512], F32)
    psC = p.ps("psC", [128, 512], F32)
    psR = p.ps("psR", [128, 128], F32)
    psQ = p.ps("psQ", [128, 1024], F32)
    psV = p.ps("psV", [128, 128], F32)

    p.dma("sp", modsc[0:1, :], mods_d, "mods", "modsc")
    p.dma("sp", ind4[0:4, :], ind4_d, "ind4d", "ind4")
    for c in range(NCH):
        p.dma("sp", xT[:, c * TOK:(c + 1) * TOK],
              x_d[c * 128:(c + 1) * 128, :], "x_d", "xT")
    h3 = aB[:, 0:NCH * TOK]
    emit_ln_affine(p, xT, "xT", h3, "h3", modsc, "modsc", 0, ones, "ln3",
                  TOK, psA, psB, psC, aF, onesrow=onesrow)
    p.phase()
    qst = aB[:, NCH * TOK + 8192:NCH * TOK + 8192 + 1024]
    sqv = aF[:, 0:1024]
    nr2 = aF[0:1, 1024:1024 + 1024]
    for wi, w in enumerate(("q", "k", "v")):
        codes_r = cW[w].rearrange("(k p) m -> p k m", p=128)
        scales_r = sW[w].rearrange("(k j) m -> j k m", j=4)
        brow = aF[0:1, 2048:2048 + D]
        p.dma("sp", brow, bW[w], f"b{w}", "brow")
        for m in range(NCH):
            p.dma("sp", sc4[0:4, 0:2048].rearrange("j (k m) -> j k m", m=128),
                  scales_r[:, :, m * 128:(m + 1) * 128], f"s{w}", "sc4")
            cod = aB[:, NCH * TOK + (m % 2) * 2048:NCH * TOK + (m % 2) * 2048 + 2048]
            ckey = f"cod{m % 2}"
            wm = aB[:, NCH * TOK + 4096 + (m % 2) * 2048:
                    NCH * TOK + 4096 + (m % 2) * 2048 + 2048]
            wmkey = f"wm{m % 2}"
            for kq in range(4):
                p.dma("pool",
                      cod.rearrange("p (k m) -> p k m", m=128)[:, 4 * kq:4 * kq + 4, :],
                      codes_r[:, 4 * kq:4 * kq + 4, m * 128:(m + 1) * 128],
                      f"c{w}", ckey)
            for k in range(NCH):
                p.mm(psR[:, :], ind4[0:4, 0:128],
                     sc4[0:4, k * 128:(k + 1) * 128],
                     reads=("ind4", "sc4"), writes="psR")
                p.tt(wm[:, k * 128:(k + 1) * 128], cod[:, k * 128:(k + 1) * 128],
                     psR[:, :], ALU.mult, reads=(ckey, "psR"), writes=wmkey)
            if w in ("q", "k"):
                for n0 in (0, 512):
                    p.mm(psQ[:, n0:n0 + 512], brow[0:1, m * 128:(m + 1) * 128],
                         onesrow[0:1, :], reads=("brow", "onesrow"),
                         writes="psQ", start=True, stop=False)
                    for k in range(NCH):
                        p.mm(psQ[:, n0:n0 + 512], wm[:, k * 128:(k + 1) * 128],
                             h3[:, k * TOK + n0:k * TOK + n0 + 512],
                             reads=(wmkey, "h3"), writes="psQ",
                             start=False, stop=(k == NCH - 1))
                p.act(qst, psQ[:, :], AF.Copy, reads="psQ", writes="qst",
                      scale=(ISQHD if w == "q" else 1.0))
                p.dma("sp", (q_o if w == "q" else k_o)[m * 128:(m + 1) * 128, :],
                      qst, "qst", "qkout")
                p.act(sqv, qst, AF.Square, reads="qst", writes="sqv")
                for n0 in (0, 512):
                    p.mm(psA[0:1, :], ones[:, 0:1], sqv[:, n0:n0 + 512],
                         reads=("sqv", "ones"), writes="psA")
                    p.act(nr2[0:1, n0:n0 + 512], psA[0:1, :], AF.Copy,
                          reads="psA", writes="nr2")
                p.reduce(nrow[0:1, wi * NHEAD + m:wi * NHEAD + m + 1],
                         nr2[0:1, 0:1024], ALU.max, reads="nr2", writes="nrow")
            else:
                for tt in range(8):
                    p.mm(psV[:, :], onesrow[0:1, 0:128],
                         brow[0:1, m * 128:(m + 1) * 128],
                         reads=("onesrow", "brow"), writes="psV",
                         start=True, stop=False)
                    for k in range(NCH):
                        p.mm(psV[:, :],
                             h3[:, k * TOK + tt * 128:k * TOK + (tt + 1) * 128],
                             wm[:, k * 128:(k + 1) * 128],
                             reads=("h3", wmkey), writes="psV",
                             start=False, stop=(k == NCH - 1))
                    p.act(qst[:, 0:128], psV[:, :], AF.Copy, reads="psV",
                          writes="qst")
                    p.dma("sp", v_o[tt * 128:(tt + 1) * 128,
                                    m * 128:(m + 1) * 128],
                          qst[:, 0:128], "qst", "vout")
    p.dma("sp", qn_o, nrow[0:1, 0:NHEAD], "nrow", "qn_o")
    p.dma("sp", kn_o, nrow[0:1, NHEAD:2 * NHEAD], "nrow", "kn_o")
    p.finish()
    return nc



def build_attn():
    nc = bass.Bass("TRN2", target_bir_lowering=False)
    q_d = nc.dram_tensor("qh", [2 * HDIM, S_FULL], BF16, kind="ExternalInput").ap()
    k_d = nc.dram_tensor("kh", [2 * HDIM, S_FULL], BF16, kind="ExternalInput").ap()
    v_d = nc.dram_tensor("vh", [S_FULL, 2 * HDIM], BF16, kind="ExternalInput").ap()
    qn_d = nc.dram_tensor("qnm", [1, 16], F32, kind="ExternalInput").ap()
    kn_d = nc.dram_tensor("knm", [1, 16], F32, kind="ExternalInput").ap()
    ao_d = nc.dram_tensor("ao", [S_FULL, 2 * HDIM], BF16, kind="ExternalOutput").ap()

    p = Prog(nc)
    ones = p.sb("ones", [128, 1], F32)
    p.memset(ones[:, :], 1.0, writes="ones")
    rows = p.sb("rows", [1, 256], F32)
    negM = p.sb("negM", [128, 2], F32)
    qt = p.sb("qt", [128, S_FULL], BF16)
    kt = p.sb("kt", [128, S_FULL], BF16)
    vt = p.sb("vt", [128, 64 * 129], BF16)
    pT = p.sb("pT", [128, 2 * 512], BF16)
    ot = p.sb("ot", [128, 128], BF16)
    rec = p.sb("rec", [128, 1], F32)
    psS = [p.ps("psS0", [128, 512], F32), p.ps("psS1", [128, 512], F32)]
    psO = [p.ps(f"psO{i}", [128, 129], F32) for i in range(4)]
    psM = p.ps("psM", [128, 2], F32)

    p.dma("sp", rows[0:1, 0:16], qn_d, "qn_d", "rows")
    p.dma("sp", rows[0:1, 16:32], kn_d, "kn_d", "rows")
    for j in range(2):
        p.reduce(rows[0:1, 32 + j:33 + j], rows[0:1, j * 8:(j + 1) * 8],
                 ALU.max, reads="rows", writes="rows")
        p.reduce(rows[0:1, 34 + j:35 + j], rows[0:1, 16 + j * 8:16 + (j + 1) * 8],
                 ALU.max, reads="rows", writes="rows")
    p.tt(rows[0:1, 36:38], rows[0:1, 32:34], rows[0:1, 34:36], ALU.mult,
         reads="rows", writes="rows")
    p.act(rows[0:1, 36:38], rows[0:1, 36:38], AF.Ln, reads="rows", writes="rows")
    p.act(rows[0:1, 36:38], rows[0:1, 36:38], AF.Exp, reads="rows",
          writes="rows", scale=0.5)
    p.memset(rows[0:1, 64:192], 1.0, writes="rows")
    for j in range(2):
        p.mm(psM[:, j:j + 1], rows[0:1, 64:192], rows[0:1, 36 + j:37 + j],
             reads=("rows",), writes="psM")
    p.act(negM[:, :], psM[:, :], AF.Copy, reads="psM", writes="negM",
          scale=-1.0)
    p.memset(vt[:, :], 1.0, writes="vt")
    for hi in range(2):
        p.dma("sp", qt[:, :], q_d[hi * 128:(hi + 1) * 128, :], "q_d", "qt")
        p.dma("sp", kt[:, :], k_d[hi * 128:(hi + 1) * 128, :], "k_d", "kt")
        for kq in range(8):
            p.dma("sp",
                  vt[:, :].rearrange("p (k c) -> p k c", c=129)[:, 8 * kq:8 * kq + 8, 0:128],
                  v_d.rearrange("(k p) d -> p k d", p=128)[:, 8 * kq:8 * kq + 8,
                                                           hi * 128:(hi + 1) * 128],
                  "v_d", "vt")
        for qb in range(16):
            q0 = qb * 512
            for kt_i in range(64):
                ps = psS[kt_i % 2]
                pskey = f"psS{kt_i % 2}"
                p.mm(ps[:, :], kt[:, kt_i * 128:(kt_i + 1) * 128],
                     qt[:, q0:q0 + 512], reads=("kt", "qt"), writes=pskey)
                pb = pT[:, (kt_i % 2) * 512:(kt_i % 2) * 512 + 512]
                pbkey = f"pT{kt_i % 2}"
                p.op("act", lambda E, o=pb, i=ps[:, :], b=negM[:, hi:hi + 1]:
                     E.activation(out=o, in_=i, func=AF.Exp, bias=b, scale=1.0),
                     reads=(pskey, "negM"), writes=(pbkey,))
                for qs in range(4):
                    p.mm(psO[qs][:, :], pb[:, qs * 128:(qs + 1) * 128],
                         vt[:, kt_i * 129:kt_i * 129 + 129],
                         reads=(pbkey, "vt"), writes=f"psO{qs}",
                         start=(kt_i == 0), stop=(kt_i == 63))
            for qs in range(4):
                p.op("dve", lambda E, o=rec[:, 0:1], i=psO[qs][:, 128:129]:
                     E.reciprocal(o, i), reads=(f"psO{qs}",), writes=("rec",))
                p.ts(ot[:, :], psO[qs][:, 0:128], rec[:, 0:1], None, ALU.mult,
                     ALU.bypass, reads=(f"psO{qs}", "rec"), writes="ot")
                p.dma("sp", ao_d[q0 + qs * 128:q0 + (qs + 1) * 128,
                                 hi * 128:(hi + 1) * 128],
                      ot[:, :], "ot", "ao")
    p.finish()
    return nc



def build_l1b():
    nc = bass.Bass("TRN2", target_bir_lowering=False)
    x_d = nc.dram_tensor("x", [D, TOK], F32, kind="ExternalInput").ap()
    at_d = nc.dram_tensor("at", [D, TOK], BF16, kind="ExternalInput").ap()
    co_d = nc.dram_tensor("co", [D, D], F32, kind="ExternalInput").ap()
    so_d = nc.dram_tensor("so", [64, D], F32, kind="ExternalInput").ap()
    bo_d = nc.dram_tensor("bo", [1, D], F32, kind="ExternalInput").ap()
    ind4_d = nc.dram_tensor("ind4", [4, 128], F32, kind="ExternalInput").ap()
    modr_d = nc.dram_tensor("modr", [1, 4 * D], F32, kind="ExternalInput").ap()
    wg_d = nc.dram_tensor("wgt", [D, HID], F32, kind="ExternalInput").ap()
    wu_d = nc.dram_tensor("wut", [D, HID], F32, kind="ExternalInput").ap()
    wd_d = nc.dram_tensor("wdt", [HID, D], F32, kind="ExternalInput").ap()
    xo_d = nc.dram_tensor("xo", [D, TOK], F32, kind="ExternalOutput").ap()
    g_d = nc.dram_tensor("g1_s", [HID, TOK], BF16, kind="Internal").ap()

    p = Prog(nc)
    ones = p.sb("ones", [128, 1], F32)
    onesrow = p.sb("onesrow", [1, 512], F32)
    p.memset(ones[:, :], 1.0, writes="ones")
    p.memset(onesrow[0:1, :], 1.0, writes="onesrow")
    xT = p.sb("xT", [128, NCH * TOK], F32)
    modsc = p.sb("modsc", [1, 2 * D], F32)
    bcols = p.sb("bcols", [128, 32], F32)
    ind4 = p.sb("ind4t", [4, 128], F32)
    sc4 = p.sb("sc4", [4, NCH * 128], F32)
    aF = p.sb("aF", [128, 9216], F32)
    aB = p.sb("aB", [128, 40960], BF16)
    psA = p.ps("psA", [1, 512], F32)
    psB = p.ps("psB", [1, 512], F32)
    psC = p.ps("psC", [128, 512], F32)
    rtc = p.ps("rtc", [128, 16], F32)
    psW = p.ps("psW", [128, 1024], F32)
    psR = p.ps("psR", [128, 128], F32)

    grow = aF[0:1, 0:2 * D]
    t1r = aF[0:1, 2 * D:3 * D]
    t2r = aF[0:1, 3 * D:4 * D]
    p.dma("sp", grow[0:1, 0:D], modr_d[0:1, 0:D], "modr", "grow")
    p.dma("sp", grow[0:1, D:2 * D], modr_d[0:1, 3 * D:4 * D], "modr", "grow")
    p.dma("sp", modsc[0:1, :], modr_d[0:1, D:3 * D], "modr", "modsc")
    p.dma("sp", ind4[0:4, :], ind4_d, "ind4d", "ind4")
    for gi in range(2):
        gseg = grow[0:1, gi * D:(gi + 1) * D]
        p.act(t1r, gseg, AF.Sign, reads="grow", writes="t1r")
        p.act(t2r, gseg, AF.Abs, reads="grow", writes="t2r")
        p.ts(t2r, t2r, 0.25, None, ALU.max, ALU.bypass, reads="t2r",
             writes="t2r")
        p.tt(gseg, t1r, t2r, ALU.mult, reads=("t1r", "t2r"), writes="grow")
        row_to_cols(p, gseg, "grow", bcols, "bcols", rtc, 16, ones,
                    off=gi * 16)
    for c in range(NCH):
        p.dma("sp", xT[:, c * TOK:(c + 1) * TOK],
              x_d[c * 128:(c + 1) * 128, :], "x_d", "xT")
    abuf = aB[:, 0:NCH * TOK]
    for c in range(NCH):
        p.dma("sp", abuf[:, c * TOK:(c + 1) * TOK],
              at_d[c * 128:(c + 1) * 128, :], "at_d", "abuf")
    p.phase()
    brow = aF[0:1, 0:D]
    p.dma("sp", brow, bo_d, "bo_d", "brow")
    codes_r = co_d.rearrange("(k p) m -> p k m", p=128)
    scales_r = so_d.rearrange("(k j) m -> j k m", j=4)
    yt = aF[:, 2048:3072]
    for m in range(NCH):
        p.dma("sp", sc4[0:4, 0:2048].rearrange("j (k m) -> j k m", m=128),
              scales_r[:, :, m * 128:(m + 1) * 128], "so_d", "sc4")
        cod = aB[:, NCH * TOK + (m % 2) * 2048:NCH * TOK + (m % 2) * 2048 + 2048]
        ckey = f"cod{m % 2}"
        wm = aB[:, NCH * TOK + 4096 + (m % 2) * 2048:
                NCH * TOK + 4096 + (m % 2) * 2048 + 2048]
        wmkey = f"wm{m % 2}"
        for kq in range(4):
            p.dma("pool",
                  cod.rearrange("p (k m) -> p k m", m=128)[:, 4 * kq:4 * kq + 4, :],
                  codes_r[:, 4 * kq:4 * kq + 4, m * 128:(m + 1) * 128],
                  "co_d", ckey)
        for k in range(NCH):
            p.mm(psR[:, :], ind4[0:4, 0:128],
                 sc4[0:4, k * 128:(k + 1) * 128],
                 reads=("ind4", "sc4"), writes="psR")
            p.tt(wm[:, k * 128:(k + 1) * 128], cod[:, k * 128:(k + 1) * 128],
                 psR[:, :], ALU.mult, reads=(ckey, "psR"), writes=wmkey)
        for n0 in (0, 512):
            p.mm(psW[:, n0:n0 + 512], brow[0:1, m * 128:(m + 1) * 128],
                 onesrow[0:1, :], reads=("brow", "onesrow"), writes="psW",
                 start=True, stop=False)
            for k in range(NCH):
                p.mm(psW[:, n0:n0 + 512], wm[:, k * 128:(k + 1) * 128],
                     abuf[:, k * TOK + n0:k * TOK + n0 + 512],
                     reads=(wmkey, "abuf"), writes="psW",
                     start=False, stop=(k == NCH - 1))
        p.act(yt, psW[:, :], AF.Copy, reads="psW", writes="yt")
        p.stt(xT[:, m * TOK:(m + 1) * TOK], yt, bcols[:, m:m + 1],
              xT[:, m * TOK:(m + 1) * TOK], ALU.mult, ALU.add,
              reads=("yt", "bcols", "xT"), writes="xT")
    p.phase()
    emit_ln_affine(p, xT, "xT", abuf, "abuf", modsc, "modsc", 0, ones,
                  "ln4", TOK, psA, psB, psC, aF)
    p.phase()
    wgr = wg_d.rearrange("(k p) h -> p k h", p=128)
    wur = wu_d.rearrange("(k p) h -> p k h", p=128)
    sg = aF[:, 0:1024]
    su = aF[:, 1024:2048]
    ght = aB[:, NCH * TOK + 8192:NCH * TOK + 8192 + 1024]
    for hc in range(64):
        wgt = aB[:, NCH * TOK + (hc % 2) * 2048:NCH * TOK + (hc % 2) * 2048 + 2048]
        wut = aB[:, NCH * TOK + 4096 + (hc % 2) * 2048:
                 NCH * TOK + 4096 + (hc % 2) * 2048 + 2048]
        gkey, ukey = f"wgt{hc % 2}", f"wut{hc % 2}"
        for kq in range(4):
            p.dma("pool",
                  wgt.rearrange("p (k h) -> p k h", h=128)[:, 4 * kq:4 * kq + 4, :],
                  wgr[:, 4 * kq:4 * kq + 4, hc * 128:(hc + 1) * 128],
                  "wg_d", gkey)
            p.dma("pool",
                  wut.rearrange("p (k h) -> p k h", h=128)[:, 4 * kq:4 * kq + 4, :],
                  wur[:, 4 * kq:4 * kq + 4, hc * 128:(hc + 1) * 128],
                  "wu_d", ukey)
        for n0 in (0, 512):
            for k in range(NCH):
                p.mm(psW[:, n0:n0 + 512], wgt[:, k * 128:(k + 1) * 128],
                     abuf[:, k * TOK + n0:k * TOK + n0 + 512],
                     reads=(gkey, "abuf"), writes="psW",
                     start=(k == 0), stop=(k == NCH - 1))
            for k in range(NCH):
                p.mm(psC[:, :], wut[:, k * 128:(k + 1) * 128],
                     abuf[:, k * TOK + n0:k * TOK + n0 + 512],
                     reads=(ukey, "abuf"), writes="psC",
                     start=(k == 0), stop=(k == NCH - 1))
            p.act(sg[:, n0:n0 + 512], psW[:, n0:n0 + 512], AF.Silu,
                  reads="psW", writes="sg")
            p.act(su[:, n0:n0 + 512], psC[:, :], AF.Copy, reads="psC",
                  writes="su")
        p.tt(ght, sg, su, ALU.mult, reads=("sg", "su"), writes="ght")
        p.dma("sp", g_d[hc * 128:(hc + 1) * 128, :], ght, "ght", "g_d")
    p.phase()
    wdr = wd_d.rearrange("(k p) d -> p k d", p=128)
    for th in range(2):
        gch = aB[:, 0:64 * 512]
        for k in range(64):
            p.dma("sp", gch[:, k * 512:(k + 1) * 512],
                  g_d[k * 128:(k + 1) * 128, th * 512:(th + 1) * 512],
                  "g_d", "gch")
        for d in range(NCH):
            wdb = aB[:, 64 * 512:64 * 512 + HID]
            wkey = "wdb"
            for kq in range(8):
                p.dma("pool",
                      wdb.rearrange("p (k m) -> p k m", m=128)[:, 8 * kq:8 * kq + 8, :],
                      wdr[:, 8 * kq:8 * kq + 8, d * 128:(d + 1) * 128],
                      "wd_d", wkey)
            for k in range(64):
                p.mm(psW[:, 0:512], wdb[:, k * 128:(k + 1) * 128],
                     gch[:, k * 512:(k + 1) * 512], reads=(wkey, "gch"),
                     writes="psW", start=(k == 0), stop=(k == 63))
            p.stt(xT[:, d * TOK + th * 512:d * TOK + th * 512 + 512],
                  psW[:, 0:512], bcols[:, 16 + d:17 + d],
                  xT[:, d * TOK + th * 512:d * TOK + th * 512 + 512],
                  ALU.mult, ALU.add, reads=("psW", "bcols", "xT"), writes="xT")
    for c in range(NCH):
        p.dma("sp", xo_d[c * 128:(c + 1) * 128, :],
              xT[:, c * TOK:(c + 1) * TOK], "xT", "xo")
    p.finish()
    return nc



def _cc(a):
    return np.ascontiguousarray(a)


def kernel_unfused(**inp):
    CORES = list(range(NCORE))
    f32 = np.float32

    cols = 6 * D // NCORE
    nc = build_mod(cols)
    im = [{"c0": inp["L0_c"], "c1": inp["L1_c"],
           "w0": _cc(inp["L0_wmod"][:, r * cols:(r + 1) * cols]),
           "w1": _cc(inp["L1_wmod"][:, r * cols:(r + 1) * cols])}
          for r in range(NCORE)]
    res = run_bass_kernel_spmd(nc, im, core_ids=CORES)
    mod0 = np.concatenate([res.results[r]["o0"] for r in range(NCORE)], axis=1)
    mod1 = np.concatenate([res.results[r]["o1"] for r in range(NCORE)], axis=1)

    x_fm = _cc(inp["L0_attn_x"][0].astype(f32).T)

    nc = build_l0a()
    wgx2 = _cc(inp["L0_attn_Wg_x"].reshape(D, 128))
    wga2 = _cc(inp["L0_attn_Wg_a"].reshape(D, 128))
    mod0a = _cc(mod0[:, 0:2 * D])
    im = []
    for r in range(NCORE):
        ts_ = slice(r * TOK, (r + 1) * TOK)
        im.append({"x": _cc(x_fm[:, ts_]), "mod0": mod0a,
                   "win": inp["L0_attn_W_in"],
                   "binr": inp["L0_attn_b_in"][None, :],
                   "wgx": wgx2, "wga": wga2,
                   "bgx": inp["L0_attn_bg_x"][None, :],
                   "bga": inp["L0_attn_bg_a"][None, :],
                   "ar": inp["L0_attn_a_real_param"][None, :],
                   "ai": inp["L0_attn_a_imag_param"][None, :]})
    res = run_bass_kernel_spmd(nc, im, core_ids=CORES)
    m_f = np.concatenate([res.results[r]["m_o"] for r in range(NCORE)], axis=1)
    th_f = np.concatenate([res.results[r]["th_o"] for r in range(NCORE)], axis=1)
    xr_f = np.concatenate([res.results[r]["xr_o"] for r in range(NCORE)], axis=1)
    xi_f = np.concatenate([res.results[r]["xi_o"] for r in range(NCORE)], axis=1)

    nc = build_scan()
    im = []
    for r in range(NCORE):
        cs = slice(r * 128, (r + 1) * 128)
        im.append({"m_i": _cc(m_f[cs]), "th_i": _cc(th_f[cs]),
                   "xr_i": _cc(xr_f[cs]), "xi_i": _cc(xi_f[cs])})
    res = run_bass_kernel_spmd(nc, im, core_ids=CORES)
    hr_f = np.concatenate([np.asarray(res.results[r]["hr_o"]).view(BF)
                           for r in range(NCORE)], axis=0)
    hi_f = np.concatenate([np.asarray(res.results[r]["hi_o"]).view(BF)
                           for r in range(NCORE)], axis=0)
    h_fm = np.concatenate([hr_f, hi_f], axis=0)

    nc = build_l0b()
    modr0 = _cc(mod0[:, 2 * D:6 * D])
    im = []
    for r in range(NCORE):
        ts_ = slice(r * TOK, (r + 1) * TOK)
        im.append({"x": _cc(x_fm[:, ts_]), "h": _cc(h_fm[:, ts_]),
                   "wout": inp["L0_attn_W_out"],
                   "boutr": inp["L0_attn_b_out"][None, :], "modr": modr0,
                   "wg": inp["L0_ffn_w_gate"], "wu": inp["L0_ffn_w_up"],
                   "wd": inp["L0_ffn_w_down"]})
    res = run_bass_kernel_spmd(nc, im, core_ids=CORES)
    x2_fm = np.concatenate([res.results[r]["x2"] for r in range(NCORE)], axis=1)

    nc = build_l1a()
    ind4 = np.zeros((4, 128), f32)
    for j in range(4):
        ind4[j, j * 32:(j + 1) * 32] = 1.0
    mods1 = _cc(mod1[:, 0:2 * D])
    im = []
    for r in range(NCORE):
        ts_ = slice(r * TOK, (r + 1) * TOK)
        im.append({"x": _cc(x2_fm[:, ts_]), "mods": mods1, "ind4": ind4,
                   "cq": inp["L1_attn_wq_codes"], "sq": inp["L1_attn_wq_scales"],
                   "bq": inp["L1_attn_bq"][None, :],
                   "ck": inp["L1_attn_wk_codes"], "sk": inp["L1_attn_wk_scales"],
                   "bk": inp["L1_attn_bk"][None, :],
                   "cv": inp["L1_attn_wv_codes"], "sv": inp["L1_attn_wv_scales"],
                   "bv": inp["L1_attn_bv"][None, :]})
    res = run_bass_kernel_spmd(nc, im, core_ids=CORES)
    q_f = np.concatenate([np.asarray(res.results[r]["q_o"]).view(BF)
                          for r in range(NCORE)], axis=1)
    k_f = np.concatenate([np.asarray(res.results[r]["k_o"]).view(BF)
                          for r in range(NCORE)], axis=1)
    v_f = np.concatenate([np.asarray(res.results[r]["v_o"]).view(BF)
                          for r in range(NCORE)], axis=0)
    qn = np.stack([res.results[r]["qn_o"][0] for r in range(NCORE)])
    kn = np.stack([res.results[r]["kn_o"][0] for r in range(NCORE)])

    nc = build_attn()
    im = []
    for r in range(NCORE):
        hs = slice(2 * r * 128, (2 * r + 2) * 128)
        qnm = np.zeros((1, 16), f32)
        knm = np.zeros((1, 16), f32)
        for j in range(2):
            qnm[0, j * 8:(j + 1) * 8] = qn[:, 2 * r + j]
            knm[0, j * 8:(j + 1) * 8] = kn[:, 2 * r + j]
        im.append({"qh": _cc(q_f[hs]), "kh": _cc(k_f[hs]),
                   "vh": _cc(v_f[:, hs]), "qnm": qnm, "knm": knm})
    res = run_bass_kernel_spmd(nc, im, core_ids=CORES)
    ao = np.concatenate([np.asarray(res.results[r]["ao"]).view(BF)
                         for r in range(NCORE)], axis=1)

    nc = build_l1b()
    at_fm = _cc(ao.T)
    modr1 = _cc(mod1[:, 2 * D:6 * D])
    wgt = _cc(inp["L1_ffn_w_gate"].T)
    wut = _cc(inp["L1_ffn_w_up"].T)
    wdt = _cc(inp["L1_ffn_w_down"].T)
    im = []
    for r in range(NCORE):
        ts_ = slice(r * TOK, (r + 1) * TOK)
        im.append({"x": _cc(x2_fm[:, ts_]), "at": _cc(at_fm[:, ts_]),
                   "co": inp["L1_attn_wo_codes"], "so": inp["L1_attn_wo_scales"],
                   "bo": inp["L1_attn_bo"][None, :], "ind4": ind4,
                   "modr": modr1, "wgt": wgt, "wut": wut, "wdt": wdt})
    res = run_bass_kernel_spmd(nc, im, core_ids=CORES)
    x4_fm = np.concatenate([res.results[r]["xo"] for r in range(NCORE)], axis=1)
    return _cc(x4_fm.T)[None].astype(np.float32)



def build_fused():
    nc = bass.Bass("TRN2", target_bir_lowering=False)
    MC = 6 * D // NCORE
    x_d = nc.dram_tensor("x", [D, TOK], F32, kind="ExternalInput").ap()
    c0_d = nc.dram_tensor("c0", [1, D], F32, kind="ExternalInput").ap()
    c1_d = nc.dram_tensor("c1", [1, D], F32, kind="ExternalInput").ap()
    wm0_d = nc.dram_tensor("wm0", [D, MC], F32, kind="ExternalInput").ap()
    wm1_d = nc.dram_tensor("wm1", [D, MC], F32, kind="ExternalInput").ap()
    alpha_d = nc.dram_tensor("alphac", [128, 7], F32, kind="ExternalInput").ap()
    ind4_d = nc.dram_tensor("ind4", [4, 128], F32, kind="ExternalInput").ap()
    idbf_d = nc.dram_tensor("idbf", [128, 128], BF16, kind="ExternalInput").ap()
    win_d = nc.dram_tensor("win", [D, D], F32, kind="ExternalInput").ap()
    bin_d = nc.dram_tensor("binr", [1, D], F32, kind="ExternalInput").ap()
    wgx_d = nc.dram_tensor("wgx", [D, 128], F32, kind="ExternalInput").ap()
    wga_d = nc.dram_tensor("wga", [D, 128], F32, kind="ExternalInput").ap()
    bgx_d = nc.dram_tensor("bgx", [1, DH], F32, kind="ExternalInput").ap()
    bga_d = nc.dram_tensor("bga", [1, DH], F32, kind="ExternalInput").ap()
    ar_d = nc.dram_tensor("ar", [1, DH], F32, kind="ExternalInput").ap()
    ai_d = nc.dram_tensor("ai", [1, DH], F32, kind="ExternalInput").ap()
    wout_d = nc.dram_tensor("wout", [D, D], F32, kind="ExternalInput").ap()
    bout_d = nc.dram_tensor("boutr", [1, D], F32, kind="ExternalInput").ap()
    wg_d = nc.dram_tensor("wg", [D, HID], F32, kind="ExternalInput").ap()
    wu_d = nc.dram_tensor("wu", [D, HID], F32, kind="ExternalInput").ap()
    wd_d = nc.dram_tensor("wd", [HID, D], F32, kind="ExternalInput").ap()
    cW, sW, bW = {}, {}, {}
    for w in ("q", "k", "v", "o"):
        cW[w] = nc.dram_tensor(f"c{w}", [D, D], F32, kind="ExternalInput").ap()
        sW[w] = nc.dram_tensor(f"s{w}", [64, D], F32, kind="ExternalInput").ap()
        bW[w] = nc.dram_tensor(f"b{w}", [1, D], F32, kind="ExternalInput").ap()
    wgt_d = nc.dram_tensor("wgt", [D, HID], F32, kind="ExternalInput").ap()
    wut_d = nc.dram_tensor("wut", [D, HID], F32, kind="ExternalInput").ap()
    wdt_d = nc.dram_tensor("wdt", [HID, D], F32, kind="ExternalInput").ap()
    xo_d = nc.dram_tensor("xo", [D, TOK], F32, kind="ExternalOutput").ap()
    m0p = nc.dram_tensor("m0p", [1, MC], F32, kind="Internal").ap()
    m1p = nc.dram_tensor("m1p", [1, MC], F32, kind="Internal").ap()
    m0g = nc.dram_tensor("m0g", [NCORE, MC], F32, kind="Internal").ap()
    m1g = nc.dram_tensor("m1g", [NCORE, MC], F32, kind="Internal").ap()
    carp = nc.dram_tensor("carp", [128, 40], F32, kind="Internal").ap()
    carg = nc.dram_tensor("carg", [NCORE * 128, 40], F32, kind="Internal").ap()
    st_ur = nc.dram_tensor("st_ur", [DH, TOK], F32, kind="Internal").ap()
    st_ui = nc.dram_tensor("st_ui", [DH, TOK], F32, kind="Internal").ap()
    st_ec = nc.dram_tensor("st_ec", [DH, TOK], F32, kind="Internal").ap()
    st_cs = nc.dram_tensor("st_cs", [DH, TOK], F32, kind="Internal").ap()
    st_sn = nc.dram_tensor("st_sn", [DH, TOK], F32, kind="Internal").ap()
    hh_s = nc.dram_tensor("hh_s", [TOK, HID], F32, kind="Internal").ap()
    g_s = nc.dram_tensor("g_s", [TOK, HID], BF16, kind="Internal").ap()
    kvp = nc.dram_tensor("kvp", [2 * D, TOK], BF16, kind="Internal").ap()
    npay = nc.dram_tensor("npay", [1, 32], F32, kind="Internal").ap()
    kvg = nc.dram_tensor("kvg", [NCORE * 2 * D, TOK], BF16, kind="Internal").ap()
    ng = nc.dram_tensor("ng", [NCORE, 32], F32, kind="Internal").ap()
    ao_s = nc.dram_tensor("ao_s", [TOK, D], BF16, kind="Internal").ap()
    g1_s = nc.dram_tensor("g1_s", [HID, TOK], BF16, kind="Internal").ap()

    p = Prog(nc)
    ones = p.sb("ones", [128, 1], F32)
    onesrow = p.sb("onesrow", [1, 512], F32)
    p.memset(ones[:, :], 1.0, writes="ones")
    p.memset(onesrow[0:1, :], 1.0, writes="onesrow")
    PF = p.sb("PF", [128, 25600], F32)
    PB = p.sb("PB", [128, 45056], BF16)
    R = p.sb("R", [1, 4096], F32)
    bcols = p.sb("bcols", [128, 64], F32)
    idbf = p.sb("idbft", [128, 128], BF16)
    ind4 = p.sb("ind4t", [4, 128], F32)
    alph = p.sb("alph", [128, 7], F32)
    psA = p.ps("psA", [1, 512], F32)
    psC = p.ps("psC", [128, 512], F32)
    stk01 = contextlib.ExitStack()
    psB = p.ps("psB01", [1, 512], F32, scope=stk01)
    rtc = p.ps("rtc01", [128, 16], F32, scope=stk01)
    psW = p.ps("psW01", [128, 1024], F32, scope=stk01)
    xres = PF[:, 0:16384]
    fscr = PF[:, 16384:25600]
    p.dma("sp", idbf[:, :], idbf_d, "idbf_d", "idbf")
    p.dma("sp", ind4[0:4, :], ind4_d, "ind4d", "ind4t")
    p.dma("sp", alph[:, :], alpha_d, "alpha_d", "alph")

    ct = R[0:1, 0:2 * D]
    p.dma("sp", ct[0:1, 0:D], c0_d, "c0", "ct")
    p.dma("sp", ct[0:1, D:2 * D], c1_d, "c1", "ct")
    p.act(ct, ct, AF.Silu, reads="ct", writes="ct")
    scols = bcols[:, 32:64]
    row_to_cols(p, ct[0:1, 0:D], "ct", scols, "scols", rtc, 16, ones)
    row_to_cols(p, ct[0:1, D:2 * D], "ct", scols, "scols", rtc, 16, ones,
                off=16)
    wmt = PF[:, 0:1024]
    mot = PF[0:1, 1024:1024 + 2 * MC]
    for li, wsrc in ((0, wm0_d), (1, wm1_d)):
        for n0 in range(0, MC, 512):
            n1 = min(n0 + 512, MC)
            for k in range(NCH):
                wb = wmt[:, (k % 2) * 512:(k % 2) * 512 + (n1 - n0)]
                wkey = f"wmt{k % 2}"
                p.dma("sp", wb, wsrc[k * 128:(k + 1) * 128, n0:n1],
                      f"wm{li}", wkey)
                p.mm(psA[0:1, 0:n1 - n0], scols[:, li * 16 + k:li * 16 + k + 1],
                     wb, reads=("scols", wkey), writes="psA",
                     start=(k == 0), stop=(k == NCH - 1))
            p.act(mot[0:1, li * MC + n0:li * MC + n1], psA[0:1, 0:n1 - n0],
                  AF.Copy, reads="psA", writes="mot")
    p.dma("sp", m0p, mot[0:1, 0:MC], "mot", "m0p")
    p.dma("sp", m1p, mot[0:1, MC:2 * MC], "mot", "m1p")
    p.collective("AllGather", [m0p], [m0g], rkeys="m0p", wkeys="m0g")
    p.collective("AllGather", [m1p], [m1g], rkeys="m1p", wkeys="m1g")
    mod0r = m0g.rearrange("r c -> (r c)")
    mod1r = m1g.rearrange("r c -> (r c)")
    p.phase()

    p.dma("sp", R[0:1, 0:D], mod0r[0:D], "m0g", "R")
    p.dma("sp", R[0:1, D:2 * D], mod0r[D:2 * D], "m0g", "R")
    s = contextlib.ExitStack()
    h1 = PB[:, 0:16384]
    xh = PB[:, 16384:32768]
    wbuf = PB[:, 32768:36864]
    wg2 = PB[:, 36864:40960]
    sq = fscr[:, 0:512]
    tmp = fscr[:, 512:1024]
    srow = fscr[0:1, 1024:1024 + 2 * TOK]
    stat = fscr[0:1, 1024 + 2 * TOK:1024 + 5 * TOK]
    carow = fscr[0:1, 1024 + 5 * TOK:1024 + 5 * TOK + D]
    var = fscr[0:1, 1024 + 5 * TOK + D:1024 + 6 * TOK + D]
    xa = PF[:, 0:2048]
    p.ts(carow, R[0:1, D:2 * D], 1.0, None, ALU.add, ALU.bypass,
         reads="R", writes="ln1")
    p.memset(stat[0:1, 2 * TOK:3 * TOK], 1.0, writes="ln1")
    for n0 in range(0, TOK, 512):
        for c in range(NCH):
            xb = xa[:, (c % 2) * 1024:(c % 2) * 1024 + 512]
            xkey = f"xa{c % 2}"
            p.dma("sp", xb, x_d[c * 128:(c + 1) * 128, n0:n0 + 512], "x_d", xkey)
            p.mm(psA[0:1, :], ones[:, 0:1], xb, reads=(xkey, "ones"),
                 writes="psA", start=(c == 0), stop=(c == NCH - 1))
            p.act(sq, xb, AF.Square, reads=xkey, writes="sq1")
            p.mm(psB[0:1, :], ones[:, 0:1], sq, reads=("sq1", "ones"),
                 writes="psB", start=(c == 0), stop=(c == NCH - 1))
        p.act(srow[0:1, n0:n0 + 512], psA[0:1, :], AF.Copy, reads="psA",
              writes="ln1", scale=1.0 / D)
        p.act(srow[0:1, TOK + n0:TOK + n0 + 512], psB[0:1, :], AF.Copy,
              reads="psB", writes="ln1", scale=1.0 / D)
    mean = srow[0:1, 0:TOK]
    msq = srow[0:1, TOK:2 * TOK]
    p.tt(var, mean, mean, ALU.mult, reads="ln1", writes="ln1v")
    p.tt(var, msq, var, ALU.subtract, reads=("ln1", "ln1v"), writes="ln1v")
    p.act(var, var, AF.Ln, reads="ln1v", writes="ln1v", bias=1e-5)
    p.act(stat[0:1, 0:TOK], var, AF.Exp, reads="ln1v", writes="ln1", scale=-0.5)
    p.tt(stat[0:1, TOK:2 * TOK], mean, stat[0:1, 0:TOK], ALU.mult,
         reads="ln1", writes="ln1")
    p.ts(stat[0:1, TOK:2 * TOK], stat[0:1, TOK:2 * TOK], -1.0, None, ALU.mult,
         ALU.bypass, reads="ln1", writes="ln1")
    for c in range(NCH):
        xb = xa[:, (c % 2) * 1024:(c % 2) * 1024 + 1024]
        xkey = f"xa{c % 2}"
        p.dma("sp", xb, x_d[c * 128:(c + 1) * 128, :], "x_d", xkey)
        for n0 in (0, 512):
            p.mm(psC[:, :], carow[0:1, c * 128:(c + 1) * 128],
                 stat[0:1, n0:n0 + 512], reads=("ln1",), writes="psC")
            p.tt(tmp, xb[:, n0:n0 + 512], psC[:, :], ALU.mult,
                 reads=(xkey, "psC"), writes="ln1t")
            p.mm(psC[:, :], carow[0:1, c * 128:(c + 1) * 128],
                 stat[0:1, TOK + n0:TOK + n0 + 512], reads=("ln1",),
                 writes="psC", start=True, stop=False)
            p.mm(psC[:, :], R[0:1, c * 128:(c + 1) * 128],
                 stat[0:1, 2 * TOK + n0:2 * TOK + n0 + 512],
                 reads=("R", "ln1"), writes="psC", start=False, stop=True)
            p.tt(h1[:, c * TOK + n0:c * TOK + n0 + 512], tmp, psC[:, :],
                 ALU.add, reads=("ln1t", "psC"), writes="h1")
    winr = win_d.rearrange("(k p) m -> p k m", p=128)
    brow = fscr[0:1, 0:D]
    p.phase()
    p.dma("sp", brow, bin_d, "bin_d", "brow")
    for m in range(NCH):
        wb = wbuf[:, (m % 2) * D:(m % 2) * D + D]
        wkey = f"wbuf{m % 2}"
        for kq in range(4):
            p.dma("pool",
                  wb.rearrange("p (k m) -> p k m", k=NCH)[:, 4 * kq:4 * kq + 4, :],
                  winr[:, 4 * kq:4 * kq + 4, m * 128:(m + 1) * 128],
                  "win_d", wkey)
        for n0 in (0, 512):
            p.mm(psW[:, n0:n0 + 512], brow[0:1, m * 128:(m + 1) * 128],
                 onesrow[0:1, :], reads=("brow", "onesrow"), writes="psW",
                 start=True, stop=False)
            for k in range(NCH):
                p.mm(psW[:, n0:n0 + 512], wb[:, k * 128:(k + 1) * 128],
                     h1[:, k * TOK + n0:k * TOK + n0 + 512],
                     reads=(wkey, "h1"), writes="psW",
                     start=False, stop=(k == NCH - 1))
        p.act(xh[:, m * TOK:(m + 1) * TOK], psW[:, :], AF.Copy,
              reads="psW", writes="xh")
    p.phase()
    for k in range(NCH):
        p.dma("pool", wg2[:, k * 128:(k + 1) * 128],
              wgx_d[k * 128:(k + 1) * 128, :], "wgx_d", "wg2")
        p.dma("pool", wg2[:, 2048 + k * 128:2048 + (k + 1) * 128],
              wga_d[k * 128:(k + 1) * 128, :], "wga_d", "wg2")
    rows2 = R[0:1, 0:4096]
    p.dma("sp", rows2[0:1, 0:DH], bgx_d, "bgx_d", "R2")
    p.dma("sp", rows2[0:1, DH:2 * DH], bga_d, "bga_d", "R2")
    p.dma("sp", rows2[0:1, 3 * DH:4 * DH], ai_d, "ai_d", "R2")
    spr = fscr[0:1, 0:2 * DH]
    p.dma("sp", spr[0:1, 0:DH], ar_d, "ar_d", "spr")
    p.act(spr[0:1, DH:2 * DH], spr[0:1, 0:DH], AF.Abs, reads="spr", writes="spr")
    p.act(spr[0:1, DH:2 * DH], spr[0:1, DH:2 * DH], AF.Exp, reads="spr",
          writes="spr", scale=-1.0)
    p.act(spr[0:1, DH:2 * DH], spr[0:1, DH:2 * DH], AF.Ln, reads="spr",
          writes="spr", bias=1.0)
    p.act(spr[0:1, 0:DH], spr[0:1, 0:DH], AF.Relu, reads="spr", writes="spr")
    p.tt(rows2[0:1, 2 * DH:3 * DH], spr[0:1, DH:2 * DH], spr[0:1, 0:DH],
         ALU.add, reads="spr", writes="R2")
    row_to_cols(p, rows2[0:1, 0:DH], "R2", bcols, "bcolg", rtc, 8, ones, off=0)
    row_to_cols(p, rows2[0:1, DH:2 * DH], "R2", bcols, "bcolg", rtc, 8, ones,
                off=8)
    row_to_cols(p, rows2[0:1, 2 * DH:3 * DH], "R2", bcols, "bcolg", rtc, 8,
                ones, off=16, scale=-8.0)
    row_to_cols(p, rows2[0:1, 3 * DH:4 * DH], "R2", bcols, "bcolg", rtc, 8,
                ones, off=24)
    p.phase()
    gx = PF[:, 0:1024]
    ga = PF[:, 1024:2048]
    la = PF[:, 2048:3072]
    th = PF[:, 3072:4096]
    sn = PF[:, 4096:5120]
    cs = PF[:, 5120:6144]
    t1 = PF[:, 6144:7168]
    t2 = PF[:, 7168:8192]
    ur = PF[:, 8192:9216]
    ui = PF[:, 9216:10240]
    eC = PF[:, 10240:11264]
    ost = PF[:, 11264:12288]
    summ = PF[:, 12288:12328]
    p.memset(ost[:, :], 1.0, writes="ost")
    for b in range(8):
        for gi, (gt_, gkey, boff) in enumerate(((gx, "gx", 0), (ga, "gag", 8))):
            woff = gi * 2048
            for n0 in (0, 512):
                for kk in (2 * b, 2 * b + 1):
                    p.mm(psW[:, n0:n0 + 512],
                         wg2[:, woff + kk * 128:woff + (kk + 1) * 128],
                         xh[:, kk * TOK + n0:kk * TOK + n0 + 512],
                         reads=("wg2", "xh"), writes="psW",
                         start=(kk == 2 * b), stop=(kk == 2 * b + 1))
            p.act(gt_[:, :], psW[:, :], AF.Sigmoid, reads=("psW", "bcolg"),
                  writes=gkey, bias=bcols[:, boff + b:boff + b + 1])
        p.ts(la[:, :], ga, bcols[:, 16 + b:17 + b], None, ALU.mult,
             ALU.bypass, reads=("gag", "bcolg"), writes="la")
        p.ts(th[:, :], ga, bcols[:, 24 + b:25 + b], None, ALU.mult,
             ALU.bypass, reads=("gag", "bcolg"), writes="th")
        p.act(t1[:, :], la, AF.Exp, reads="la", writes="t1", scale=2.0)
        p.act(t1[:, :], t1, AF.Ln, reads="t1", writes="t1", scale=-1.0,
              bias=1.0)
        p.act(t1[:, :], t1, AF.Exp, reads="t1", writes="t1", scale=0.5)
        p.tt(gx[:, :], gx, t1, ALU.mult, reads=("gx", "t1"), writes="gx")
        p.tt(t1[:, :], gx, xh[:, b * TOK:(b + 1) * TOK], ALU.mult,
             reads=("gx", "xh"), writes="t1")
        p.tt(t2[:, :], gx, xh[:, (8 + b) * TOK:(9 + b) * TOK], ALU.mult,
             reads=("gx", "xh"), writes="t2")
        p.scan(th[:, :], ost, th[:, :], 0.0, reads=("ost", "th"), writes="th")
        p.scan(eC[:, :], ost, la[:, :], 0.0, reads=("ost", "la"), writes="eC")
        p.act(la[:, :], la, AF.Exp, reads="la", writes="la")
        p.act(eC[:, :], eC, AF.Exp, reads="eC", writes="eC")
        p.ts(cs[:, :], th, INV2PI, None, ALU.mult, ALU.bypass, reads="th",
             writes="cs")
        p.ts(sn[:, :], cs, MAGIC, MAGIC, ALU.add, ALU.subtract, reads="cs",
             writes="sn")
        p.tt(cs[:, :], cs, sn, ALU.subtract, reads=("cs", "sn"), writes="cs")
        p.act(sn[:, :], cs, AF.Sin, reads="cs", writes="sn", scale=TWOPI)
        p.ts(cs[:, :], th, INV2PI, 0.25, ALU.mult, ALU.add, reads="th",
             writes="cs")
        p.ts(th[:, :], cs, MAGIC, MAGIC, ALU.add, ALU.subtract, reads="cs",
             writes="th")
        p.tt(cs[:, :], cs, th, ALU.subtract, reads=("cs", "th"), writes="cs")
        p.act(cs[:, :], cs, AF.Sin, reads="cs", writes="cs", scale=TWOPI)
        p.tt(ur[:, :], sn, t2, ALU.mult, reads=("sn", "t2"), writes="ur")
        p.tt(ui[:, :], sn, t1, ALU.mult, reads=("sn", "t1"), writes="ui")
        p.tt(t1[:, :], cs, t1, ALU.mult, reads=("cs", "t1"), writes="t1")
        p.tt(t2[:, :], cs, t2, ALU.mult, reads=("cs", "t2"), writes="t2")
        p.tt(t1[:, :], t1, ur, ALU.add, reads=("t1", "ur"), writes="t1")
        p.tt(t2[:, :], t2, ui, ALU.subtract, reads=("t2", "ui"), writes="t2")
        p.scan(ur[:, :], la, t1, 0.0, reads=("la", "t1"), writes="ur")
        p.scan(ui[:, :], la, t2, 0.0, reads=("la", "t2"), writes="ui")
        p.dma("sp", st_ur[b * 128:(b + 1) * 128, :], ur, "ur", "st")
        p.dma("sp", st_ui[b * 128:(b + 1) * 128, :], ui, "ui", "st")
        p.dma("sp", st_ec[b * 128:(b + 1) * 128, :], eC, "eC", "st")
        p.dma("sp", st_cs[b * 128:(b + 1) * 128, :], cs, "cs", "st")
        p.dma("sp", st_sn[b * 128:(b + 1) * 128, :], sn, "sn", "st")
        for qi, tile in enumerate((eC, cs, sn, ur, ui)):
            p.copy(summ[:, qi * 8 + b:qi * 8 + b + 1], tile[:, TOK - 1:TOK],
                   reads={0: "eC", 1: "cs", 2: "sn", 3: "ur", 4: "ui"}[qi],
                   writes="summ")
    p.dma("sp", carp, summ[:, :], "summ", "carp")
    p.collective("AllGather", [carp], [carg], rkeys="carp", wkeys="carg")
    p.phase()
    Hr = PF[:, 0:8]
    Hi = PF[:, 8:16]
    jb = PF[:, 16:56]
    tr_ = PF[:, 56:64]
    ti_ = PF[:, 64:72]
    t3_ = PF[:, 72:80]
    t4_ = PF[:, 80:88]
    p.memset(Hr, 0.0, writes="Hr")
    p.memset(Hi, 0.0, writes="Hi")
    for j in range(7):
        p.dma("sp", jb, carg[j * 128:(j + 1) * 128, :], "carg", "jb")
        Me, csj, snj, urj, uij = (jb[:, 0:8], jb[:, 8:16], jb[:, 16:24],
                                  jb[:, 24:32], jb[:, 32:40])
        p.tt(tr_, Me, Hr, ALU.mult, reads=("jb", "Hr"), writes="tr")
        p.tt(tr_, tr_, urj, ALU.add, reads=("tr", "jb"), writes="tr")
        p.tt(ti_, Me, Hi, ALU.mult, reads=("jb", "Hi"), writes="ti")
        p.tt(ti_, ti_, uij, ALU.add, reads=("ti", "jb"), writes="ti")
        p.tt(t3_, csj, tr_, ALU.mult, reads=("jb", "tr"), writes="t3")
        p.tt(t4_, snj, ti_, ALU.mult, reads=("jb", "ti"), writes="t4")
        p.tt(t3_, t3_, t4_, ALU.subtract, reads=("t3", "t4"), writes="t3")
        p.tt(t3_, t3_, Hr, ALU.subtract, reads=("t3", "Hr"), writes="t3")
        p.stt(Hr, t3_, alph[:, j:j + 1], Hr, ALU.mult, ALU.add,
              reads=("t3", "alph", "Hr"), writes="Hr")
        p.tt(t3_, snj, tr_, ALU.mult, reads=("jb", "tr"), writes="t3")
        p.tt(tr_, csj, ti_, ALU.mult, reads=("jb", "ti"), writes="tr")
        p.tt(t3_, t3_, tr_, ALU.add, reads=("t3", "tr"), writes="t3")
        p.tt(t3_, t3_, Hi, ALU.subtract, reads=("t3", "Hi"), writes="t3")
        p.stt(Hi, t3_, alph[:, j:j + 1], Hi, ALU.mult, ALU.add,
              reads=("t3", "alph", "Hi"), writes="Hi")
    urT = PF[:, 1024:2048]
    uiT = PF[:, 2048:3072]
    ecT = PF[:, 3072:4096]
    csT = PF[:, 4096:5120]
    snT = PF[:, 5120:6144]
    w1 = PF[:, 6144:7168]
    w2 = PF[:, 7168:8192]
    hfm = PB[:, 0:16384]
    for b in range(8):
        p.dma("sp", urT, st_ur[b * 128:(b + 1) * 128, :], "st", "urT")
        p.dma("sp", uiT, st_ui[b * 128:(b + 1) * 128, :], "st", "uiT")
        p.dma("sp", ecT, st_ec[b * 128:(b + 1) * 128, :], "st", "ecT")
        p.dma("sp", csT, st_cs[b * 128:(b + 1) * 128, :], "st", "csT")
        p.dma("sp", snT, st_sn[b * 128:(b + 1) * 128, :], "st", "snT")
        p.stt(urT, ecT, Hr[:, b:b + 1], urT, ALU.mult, ALU.add,
              reads=("ecT", "Hr", "urT"), writes="urT")
        p.stt(uiT, ecT, Hi[:, b:b + 1], uiT, ALU.mult, ALU.add,
              reads=("ecT", "Hi", "uiT"), writes="uiT")
        p.tt(w1, csT, urT, ALU.mult, reads=("csT", "urT"), writes="w1")
        p.tt(w2, snT, uiT, ALU.mult, reads=("snT", "uiT"), writes="w2")
        p.tt(hfm[:, b * TOK:(b + 1) * TOK], w1, w2, ALU.subtract,
             reads=("w1", "w2"), writes="hfm")
        p.tt(w1, snT, urT, ALU.mult, reads=("snT", "urT"), writes="w1")
        p.tt(w2, csT, uiT, ALU.mult, reads=("csT", "uiT"), writes="w2")
        p.tt(hfm[:, (8 + b) * TOK:(9 + b) * TOK], w1, w2, ALU.add,
             reads=("w1", "w2"), writes="hfm")
    p.phase()

    stk01.close()
    stk3 = contextlib.ExitStack()
    psB = p.ps("psB3", [1, 512], F32, scope=stk3)
    rtc = p.ps("rtc3", [128, 16], F32, scope=stk3)
    psW = p.ps("psW3", [128, 1024], F32, scope=stk3)
    psG = p.ps("psG3", [128, 512], F32, scope=stk3)
    psU = p.ps("psU3", [128, 512], F32, scope=stk3)
    grow = R[0:1, 0:2 * D]
    t1r = fscr[0:1, 0:D]
    t2r = fscr[0:1, D:2 * D]
    p.dma("sp", grow[0:1, 0:D], mod0r[2 * D:3 * D], "m0g", "grow")
    p.dma("sp", grow[0:1, D:2 * D], mod0r[5 * D:6 * D], "m0g", "grow")
    for gi in range(2):
        gseg = grow[0:1, gi * D:(gi + 1) * D]
        p.act(t1r, gseg, AF.Sign, reads="grow", writes="t1r")
        p.act(t2r, gseg, AF.Abs, reads="grow", writes="t2r")
        p.ts(t2r, t2r, 0.25, None, ALU.max, ALU.bypass, reads="t2r",
             writes="t2r")
        p.tt(gseg, t1r, t2r, ALU.mult, reads=("t1r", "t2r"), writes="grow")
        row_to_cols(p, gseg, "grow", bcols, "bcolr", rtc, 16, ones,
                    off=gi * 16)
    for c in range(NCH):
        p.dma("sp", xres[:, c * TOK:(c + 1) * TOK],
              x_d[c * 128:(c + 1) * 128, :], "x_d", "xres")
    brow = fscr[0:1, 0:D]
    p.dma("sp", brow, bout_d, "bout", "brow3")
    woutr = wout_d.rearrange("(k p) m -> p k m", p=128)
    yt = fscr[:, 2048:3072]
    wbuf3 = PB[:, 16384:20480]
    for m in range(NCH):
        wb = wbuf3[:, (m % 2) * D:(m % 2) * D + D]
        wkey = f"wb3{m % 2}"
        for kq in range(4):
            p.dma("pool",
                  wb.rearrange("p (k m) -> p k m", k=NCH)[:, 4 * kq:4 * kq + 4, :],
                  woutr[:, 4 * kq:4 * kq + 4, m * 128:(m + 1) * 128],
                  "wout_d", wkey)
        for n0 in (0, 512):
            p.mm(psW[:, n0:n0 + 512], brow[0:1, m * 128:(m + 1) * 128],
                 onesrow[0:1, :], reads=("brow3", "onesrow"), writes="psW",
                 start=True, stop=False)
            for k in range(NCH):
                p.mm(psW[:, n0:n0 + 512], wb[:, k * 128:(k + 1) * 128],
                     hfm[:, k * TOK + n0:k * TOK + n0 + 512],
                     reads=(wkey, "hfm"), writes="psW",
                     start=False, stop=(k == NCH - 1))
        p.act(yt, psW[:, :], AF.Copy, reads="psW", writes="yt")
        p.stt(xres[:, m * TOK:(m + 1) * TOK], yt, bcols[:, m:m + 1],
              xres[:, m * TOK:(m + 1) * TOK], ALU.mult, ALU.add,
              reads=("yt", "bcolr", "xres"), writes="xres")
    p.phase()
    p.dma("sp", R[0:1, 0:2 * D], mod0r[3 * D:5 * D], "m0g", "Rln2")
    h2 = PB[:, 0:16384]
    emit_ln_affine(p, xres, "xres", h2, "h2", R, "Rln2", 0, ones, "f2",
                  TOK, psA, psB, psC, fscr, onesrow=onesrow)
    p.phase()
    wgr = wg_d.rearrange("(k p) h -> p k h", p=128)
    wur = wu_d.rearrange("(k p) h -> p k h", p=128)
    sg = fscr[:, 0:512]
    su = fscr[:, 512:1024]
    shh = fscr[:, 1024:1536]
    for ht in range(16):
        wgt3 = PB[:, 16384:16384 + 8192]
        wut3 = PB[:, 24576:24576 + 8192]
        for kq in range(4):
            p.dma("pool",
                  wgt3.rearrange("p (k h) -> p k h", h=512)[:, 4 * kq:4 * kq + 4, :],
                  wgr[:, 4 * kq:4 * kq + 4, ht * 512:(ht + 1) * 512],
                  "wg_d", "wgt3")
            p.dma("pool",
                  wut3.rearrange("p (k h) -> p k h", h=512)[:, 4 * kq:4 * kq + 4, :],
                  wur[:, 4 * kq:4 * kq + 4, ht * 512:(ht + 1) * 512],
                  "wu_d", "wut3")
        for tt in range(8):
            for k in range(NCH):
                p.mm(psG[:, :], h2[:, k * TOK + tt * 128:k * TOK + (tt + 1) * 128],
                     wgt3[:, k * 512:(k + 1) * 512], reads=("h2", "wgt3"),
                     writes="psG", start=(k == 0), stop=(k == NCH - 1))
            for k in range(NCH):
                p.mm(psU[:, :], h2[:, k * TOK + tt * 128:k * TOK + (tt + 1) * 128],
                     wut3[:, k * 512:(k + 1) * 512], reads=("h2", "wut3"),
                     writes="psU", start=(k == 0), stop=(k == NCH - 1))
            p.act(sg, psG[:, :], AF.Silu, reads="psG", writes="sg")
            p.act(su, psU[:, :], AF.Copy, reads="psU", writes="su")
            p.tt(shh, sg, su, ALU.mult, reads=("sg", "su"), writes="shh")
            p.dma("sp", hh_s[tt * 128:(tt + 1) * 128, ht * 512:(ht + 1) * 512],
                  shh, "shh", "hh_s")
    p.phase()
    SEC = 2048
    hh = fscr[:, 0:SEC]
    ha = fscr[:, SEC:2 * SEC]
    s1 = fscr[:, 2 * SEC:2 * SEC + SEC // 4]
    s2 = fscr[:, 2 * SEC + SEC // 4:2 * SEC + SEC // 2]
    tq = PB[:, 16384:16384 + 6 * (SEC // 4)]
    gt = PB[:, 16384 + 6 * (SEC // 4):16384 + 6 * (SEC // 4) + SEC]
    Q = SEC // 4
    for tt in range(8):
        for sec in range(HID // SEC):
            p.dma("sp", hh, hh_s[tt * 128:(tt + 1) * 128,
                                  sec * SEC:(sec + 1) * SEC], "hh_s", "hh")
            p.act(ha, hh, AF.Abs, reads="hh", writes="ha")
            hq = [ha.rearrange("p (g f) -> p g f", f=4)[:, :, i] for i in range(4)]
            pairs = [(0, 1), (0, 2), (0, 3), (1, 2), (1, 3), (2, 3)]
            for pi, (a, b) in enumerate(pairs):
                p.tt(tq[:, pi * Q:(pi + 1) * Q], hq[a], hq[b], ALU.is_ge,
                     reads="ha", writes="tq")
            t01, t02, t03, t12, t13, t23 = [tq[:, i * Q:(i + 1) * Q]
                                            for i in range(6)]
            hhq = [hh.rearrange("p (g f) -> p g f", f=4)[:, :, i] for i in range(4)]
            gq = [gt.rearrange("p (g f) -> p g f", f=4)[:, :, i] for i in range(4)]
            p.tt(s1, t01, t02, ALU.add, reads="tq", writes="s1")
            p.tt(s1, s1, t03, ALU.add, reads=("s1", "tq"), writes="s1")
            p.stt(gq[0], s1, 1.5, hhq[0], ALU.is_ge, ALU.mult,
                  reads=("s1", "hh"), writes="gt")
            p.tt(s2, t12, t13, ALU.add, reads="tq", writes="s2")
            p.tt(s2, s2, t01, ALU.subtract, reads=("s2", "tq"), writes="s2")
            p.stt(gq[1], s2, 0.5, hhq[1], ALU.is_ge, ALU.mult,
                  reads=("s2", "hh"), writes="gt")
            p.tt(s1, t02, t12, ALU.add, reads="tq", writes="s1")
            p.tt(s1, s1, t23, ALU.subtract, reads=("s1", "tq"), writes="s1")
            p.stt(gq[2], s1, 0.5, hhq[2], ALU.is_le, ALU.mult,
                  reads=("s1", "hh"), writes="gt")
            p.tt(s2, t03, t13, ALU.add, reads="tq", writes="s2")
            p.tt(s2, s2, t23, ALU.add, reads=("s2", "tq"), writes="s2")
            p.stt(gq[3], s2, 1.5, hhq[3], ALU.is_le, ALU.mult,
                  reads=("s2", "hh"), writes="gt")
            p.dma("sp", g_s[tt * 128:(tt + 1) * 128, sec * SEC:(sec + 1) * SEC],
                  gt, "gt", "g_s")
    p.phase()
    wdr = wd_d.rearrange("(k p) d -> p k d", p=128)
    for th_ in range(2):
        gch = PB[:, 0:64 * 512]
        for k in range(64):
            p.op("sp", lambda E, dst=gch[:, k * 512:(k + 1) * 512],
                 src=g_s[th_ * 512:(th_ + 1) * 512, k * 128:(k + 1) * 128]:
                 E.dma_start(out=dst, in_=src, transpose=True),
                 reads=("g_s",), writes=("gch",), dma=True)
        for d in range(NCH):
            wdb = PB[:, 64 * 512:64 * 512 + HID]
            for kq in range(8):
                p.dma("pool",
                      wdb.rearrange("p (k m) -> p k m", m=128)[:, 8 * kq:8 * kq + 8, :],
                      wdr[:, 8 * kq:8 * kq + 8, d * 128:(d + 1) * 128],
                      "wd_d", "wdb")
            for k in range(64):
                p.mm(psW[:, 0:512], wdb[:, k * 128:(k + 1) * 128],
                     gch[:, k * 512:(k + 1) * 512], reads=("wdb", "gch"),
                     writes="psW", start=(k == 0), stop=(k == 63))
            p.stt(xres[:, d * TOK + th_ * 512:d * TOK + th_ * 512 + 512],
                  psW[:, 0:512], bcols[:, 16 + d:17 + d],
                  xres[:, d * TOK + th_ * 512:d * TOK + th_ * 512 + 512],
                  ALU.mult, ALU.add, reads=("psW", "bcolr", "xres"),
                  writes="xres")
    p.phase()

    stk3.close()
    stk4 = contextlib.ExitStack()
    psB = p.ps("psB4", [1, 512], F32, scope=stk4)
    psW = p.ps("psW4", [128, 1024], F32, scope=stk4)
    p.dma("sp", R[0:1, 0:2 * D], mod1r[0:2 * D], "m1g", "Rln3")
    h3 = PB[:, 0:16384]
    emit_ln_affine(p, xres, "xres", h3, "h3", R, "Rln3", 0, ones, "f3",
                  TOK, psA, psB, psC, fscr, onesrow=onesrow)
    p.phase()
    qres = PB[:, 16384:32768]
    sc4 = fscr[0:4, 0:2048]
    nrow = fscr[0:1, 2048:2048 + 32]
    brow4 = fscr[0:1, 3072:3072 + D]
    sqv = fscr[:, 5120:6144]
    nr2 = fscr[0:1, 6144:7168]
    qst = PB[:, 40960:40960 + 1024]
    for wi, w in enumerate(("q", "k", "v")):
        codes_r = cW[w].rearrange("(k p) m -> p k m", p=128)
        scales_r = sW[w].rearrange("(k j) m -> j k m", j=4)
        p.dma("sp", brow4, bW[w], f"b{w}", "brow4")
        for m in range(NCH):
            p.dma("sp", sc4[0:4, 0:2048].rearrange("j (k m) -> j k m", m=128),
                  scales_r[:, :, m * 128:(m + 1) * 128], f"s{w}", "sc4")
            cod = PB[:, 32768 + (m % 2) * 2048:32768 + (m % 2) * 2048 + 2048]
            ckey = f"cod{m % 2}"
            wm = PB[:, 36864 + (m % 2) * 2048:36864 + (m % 2) * 2048 + 2048]
            wmkey = f"wm{m % 2}"
            for kq in range(4):
                p.dma("pool",
                      cod.rearrange("p (k m) -> p k m", m=128)[:, 4 * kq:4 * kq + 4, :],
                      codes_r[:, 4 * kq:4 * kq + 4, m * 128:(m + 1) * 128],
                      f"c{w}", ckey)
            for k in range(NCH):
                p.mm(psC[:, 0:128], ind4[0:4, 0:128],
                     sc4[0:4, k * 128:(k + 1) * 128],
                     reads=("ind4t", "sc4"), writes="psC")
                p.tt(wm[:, k * 128:(k + 1) * 128], cod[:, k * 128:(k + 1) * 128],
                     psC[:, 0:128], ALU.mult, reads=(ckey, "psC"), writes=wmkey)
            if w in ("q", "k"):
                for n0 in (0, 512):
                    p.mm(psW[:, n0:n0 + 512], brow4[0:1, m * 128:(m + 1) * 128],
                         onesrow[0:1, :], reads=("brow4", "onesrow"),
                         writes="psW", start=True, stop=False)
                    for k in range(NCH):
                        p.mm(psW[:, n0:n0 + 512], wm[:, k * 128:(k + 1) * 128],
                             h3[:, k * TOK + n0:k * TOK + n0 + 512],
                             reads=(wmkey, "h3"), writes="psW",
                             start=False, stop=(k == NCH - 1))
                if w == "q":
                    p.act(qres[:, m * TOK:(m + 1) * TOK], psW[:, :], AF.Copy,
                          reads="psW", writes="qres", scale=ISQHD)
                    p.act(sqv, qres[:, m * TOK:(m + 1) * TOK], AF.Square,
                          reads="qres", writes="sqv")
                else:
                    p.act(qst, psW[:, :], AF.Copy, reads="psW", writes="qst")
                    p.dma("sp", kvp[m * 128:(m + 1) * 128, :], qst,
                          "qst", "kpay")
                    p.act(sqv, qst, AF.Square, reads="qst", writes="sqv")
                for n0 in (0, 512):
                    p.mm(psA[0:1, :], ones[:, 0:1], sqv[:, n0:n0 + 512],
                         reads=("sqv", "ones"), writes="psA")
                    p.act(nr2[0:1, n0:n0 + 512], psA[0:1, :], AF.Copy,
                          reads="psA", writes="nr2")
                p.reduce(nrow[0:1, wi * NHEAD + m:wi * NHEAD + m + 1],
                         nr2[0:1, 0:1024], ALU.max, reads="nr2", writes="nrow")
            else:
                for tt in range(8):
                    p.mm(psC[:, 0:128], onesrow[0:1, 0:128],
                         brow4[0:1, m * 128:(m + 1) * 128],
                         reads=("onesrow", "brow4"), writes="psC",
                         start=True, stop=False)
                    for k in range(NCH):
                        p.mm(psC[:, 0:128],
                             h3[:, k * TOK + tt * 128:k * TOK + (tt + 1) * 128],
                             wm[:, k * 128:(k + 1) * 128],
                             reads=("h3", wmkey), writes="psC",
                             start=False, stop=(k == NCH - 1))
                    p.act(qst[:, 0:128], psC[:, 0:128], AF.Copy, reads="psC",
                          writes="qst")
                    vview = kvp[D:2 * D, :].rearrange(
                        "(t a) b -> t (a b)", a=2)
                    p.dma("sp", vview[tt * 128:(tt + 1) * 128,
                                      m * 128:(m + 1) * 128],
                          qst[:, 0:128], "qst", "kpay")
    p.dma("sp", npay, nrow[0:1, 0:32], "nrow", "npay")
    p.collective("AllGather", [kvp], [kvg], rkeys="kpay", wkeys="kvg")
    p.collective("AllGather", [npay], [ng], rkeys="npay", wkeys="ng")
    p.phase()

    stk4.close()
    stk5 = contextlib.ExitStack()
    psS = [p.ps("psS5a", [128, 512], F32, scope=stk5),
           p.ps("psS5b", [128, 512], F32, scope=stk5)]
    psO = [p.ps(f"psO5{i}", [128, 129], F32, scope=stk5) for i in range(4)]
    nrows = fscr[0:1, 0:320]
    negM = fscr[:, 512:514]
    p.dma("sp", nrows[0:1, 0:256], ng.rearrange("r c -> (r c)"), "ng", "nrows")
    p.copy(nrows[0:1, 256:288], nrows[0:1, 0:32], reads="nrows", writes="nrows")
    for r in range(1, 8):
        p.tt(nrows[0:1, 256:288], nrows[0:1, 256:288],
             nrows[0:1, r * 32:r * 32 + 32], ALU.max, reads="nrows",
             writes="nrows")
    p.tt(nrows[0:1, 288:288 + 16], nrows[0:1, 256:256 + 16],
         nrows[0:1, 272:272 + 16], ALU.mult, reads="nrows", writes="nrows")
    p.act(nrows[0:1, 288:304], nrows[0:1, 288:304], AF.Ln, reads="nrows",
          writes="nrows")
    p.act(nrows[0:1, 288:304], nrows[0:1, 288:304], AF.Exp, reads="nrows",
          writes="nrows", scale=0.5)
    kh5 = PB[:, 0:8192]
    vt5 = PB[:, 32768:32768 + 64 * 129]
    pT5 = PB[:, 41024:41024 + 1024]
    ot5 = PB[:, 42048:42048 + 128]
    rec5 = fscr[:, 514:515]
    p.memset(vt5[:, :], 1.0, writes="vt5")
    for hi in range(NHEAD):
        p.mm(psC[:, 0:1], onesrow[0:1, 0:128],
             nrows[0:1, 288 + hi:289 + hi], reads=("onesrow", "nrows"),
             writes="psC")
        p.act(negM[:, 0:1], psC[:, 0:1], AF.Copy, reads="psC", writes="negM",
              scale=-1.0)
        for r in range(NCORE):
            p.dma("sp", kh5[:, r * TOK:(r + 1) * TOK],
                  kvg[r * 2 * D + hi * 128:r * 2 * D + (hi + 1) * 128, :],
                  "kvg", "kh5")
        for kq in range(8):
            vsrc = kvg[kq * 2 * D + D:(kq + 1) * 2 * D, :].rearrange(
                "(k p f1) f2 -> p k (f1 f2)", k=8, p=128, f1=2)
            p.dma("sp",
                  vt5[:, :].rearrange("p (k c) -> p k c", c=129)[:, 8 * kq:8 * kq + 8, 0:128],
                  vsrc[:, :, hi * 128:(hi + 1) * 128],
                  "kvg", "vt5")
        for qb in range(2):
            q0 = qb * 512
            for kt_i in range(64):
                ps = psS[kt_i % 2]
                pskey = f"psS{kt_i % 2}"
                p.mm(ps[:, :], kh5[:, kt_i * 128:(kt_i + 1) * 128],
                     qres[:, hi * TOK + q0:hi * TOK + q0 + 512],
                     reads=("kh5", "qres"), writes=pskey)
                pb = pT5[:, (kt_i % 2) * 512:(kt_i % 2) * 512 + 512]
                pbkey = f"pT{kt_i % 2}"
                p.op("act", lambda E, o=pb, i=ps[:, :], b=negM[:, 0:1]:
                     E.activation(out=o, in_=i, func=AF.Exp, bias=b, scale=1.0),
                     reads=(pskey, "negM"), writes=(pbkey,))
                for qs in range(4):
                    p.mm(psO[qs][:, 0:129],
                         pb[:, qs * 128:(qs + 1) * 128],
                         vt5[:, kt_i * 129:kt_i * 129 + 129],
                         reads=(pbkey, "vt5"), writes=f"psO{qs}",
                         start=(kt_i == 0), stop=(kt_i == 63))
            for qs in range(4):
                p.op("dve", lambda E, o=rec5[:, 0:1],
                     i=psO[qs][:, 128:129]:
                     E.reciprocal(o, i), reads=(f"psO{qs}",), writes=("rec5",))
                p.ts(ot5[:, :], psO[qs][:, 0:128], rec5[:, 0:1],
                     None, ALU.mult, ALU.bypass, reads=(f"psO{qs}", "rec5"),
                     writes="ot5")
                p.dma("sp", ao_s[q0 + qs * 128:q0 + (qs + 1) * 128,
                                 hi * 128:(hi + 1) * 128],
                      ot5[:, :], "ot5", "ao_s")
    p.phase()
    atf = PB[:, 0:16384]
    for f in range(NCH):
        p.op("sp", lambda E, dst=atf[:, f * TOK:(f + 1) * TOK],
             src=ao_s[:, f * 128:(f + 1) * 128]:
             E.dma_start(out=dst, in_=src, transpose=True),
             reads=("ao_s",), writes=("atf",), dma=True)

    stk5.close()
    stk6 = contextlib.ExitStack()
    psB = p.ps("psB6", [1, 512], F32, scope=stk6)
    rtc = p.ps("rtc6", [128, 16], F32, scope=stk6)
    psW = p.ps("psW6", [128, 1024], F32, scope=stk6)
    psG = p.ps("psG6", [128, 512], F32, scope=stk6)
    psU = p.ps("psU6", [128, 512], F32, scope=stk6)
    grow = R[0:1, 0:2 * D]
    t1r = fscr[0:1, 2048:2048 + D]
    t2r = fscr[0:1, 2048 + D:2048 + 2 * D]
    p.dma("sp", grow[0:1, 0:D], mod1r[2 * D:3 * D], "m1g", "grow6")
    p.dma("sp", grow[0:1, D:2 * D], mod1r[5 * D:6 * D], "m1g", "grow6")
    for gi in range(2):
        gseg = grow[0:1, gi * D:(gi + 1) * D]
        p.act(t1r, gseg, AF.Sign, reads="grow6", writes="t1r6")
        p.act(t2r, gseg, AF.Abs, reads="grow6", writes="t2r6")
        p.ts(t2r, t2r, 0.25, None, ALU.max, ALU.bypass, reads="t2r6",
             writes="t2r6")
        p.tt(gseg, t1r, t2r, ALU.mult, reads=("t1r6", "t2r6"), writes="grow6")
        row_to_cols(p, gseg, "grow6", bcols, "bcol6", rtc, 16, ones,
                    off=gi * 16)
    brow6 = fscr[0:1, 0:D]
    p.dma("sp", brow6, bW["o"], "bo", "brow6")
    codes_r = cW["o"].rearrange("(k p) m -> p k m", p=128)
    scales_r = sW["o"].rearrange("(k j) m -> j k m", j=4)
    sc4o = fscr[0:4, 7168:7168 + 2048]
    yt6 = fscr[:, 6144:7168]
    for m in range(NCH):
        p.dma("sp", sc4o[0:4, 0:2048].rearrange("j (k m) -> j k m", m=128),
              scales_r[:, :, m * 128:(m + 1) * 128], "so", "sc4o")
        cod = PB[:, 32768 + (m % 2) * 2048:32768 + (m % 2) * 2048 + 2048]
        ckey = f"co6{m % 2}"
        wm = PB[:, 36864 + (m % 2) * 2048:36864 + (m % 2) * 2048 + 2048]
        wmkey = f"wo6{m % 2}"
        for kq in range(4):
            p.dma("pool",
                  cod.rearrange("p (k m) -> p k m", m=128)[:, 4 * kq:4 * kq + 4, :],
                  codes_r[:, 4 * kq:4 * kq + 4, m * 128:(m + 1) * 128],
                  "co", ckey)
        for k in range(NCH):
            p.mm(psC[:, 0:128], ind4[0:4, 0:128],
                 sc4o[0:4, k * 128:(k + 1) * 128],
                 reads=("ind4t", "sc4o"), writes="psC")
            p.tt(wm[:, k * 128:(k + 1) * 128], cod[:, k * 128:(k + 1) * 128],
                 psC[:, 0:128], ALU.mult, reads=(ckey, "psC"), writes=wmkey)
        for n0 in (0, 512):
            p.mm(psW[:, n0:n0 + 512], brow6[0:1, m * 128:(m + 1) * 128],
                 onesrow[0:1, :], reads=("brow6", "onesrow"), writes="psW",
                 start=True, stop=False)
            for k in range(NCH):
                p.mm(psW[:, n0:n0 + 512], wm[:, k * 128:(k + 1) * 128],
                     atf[:, k * TOK + n0:k * TOK + n0 + 512],
                     reads=(wmkey, "atf"), writes="psW",
                     start=False, stop=(k == NCH - 1))
        p.act(yt6, psW[:, :], AF.Copy, reads="psW", writes="yt6")
        p.stt(xres[:, m * TOK:(m + 1) * TOK], yt6, bcols[:, m:m + 1],
              xres[:, m * TOK:(m + 1) * TOK], ALU.mult, ALU.add,
              reads=("yt6", "bcol6", "xres"), writes="xres")
    p.phase()
    p.dma("sp", R[0:1, 0:2 * D], mod1r[3 * D:5 * D], "m1g", "Rln4")
    h4 = PB[:, 0:16384]
    emit_ln_affine(p, xres, "xres", h4, "h4", R, "Rln4", 0, ones, "f4",
                  TOK, psA, psB, psC, fscr, onesrow=onesrow)
    p.phase()
    wgr1 = wgt_d.rearrange("(k p) h -> p k h", p=128)
    wur1 = wut_d.rearrange("(k p) h -> p k h", p=128)
    sg6 = fscr[:, 0:1024]
    su6 = fscr[:, 1024:2048]
    ght = PB[:, 40960:40960 + 1024]
    for hc in range(64):
        wgt6 = PB[:, 16384 + (hc % 2) * 2048:16384 + (hc % 2) * 2048 + 2048]
        wut6 = PB[:, 20480 + (hc % 2) * 2048:20480 + (hc % 2) * 2048 + 2048]
        gkey, ukey = f"wgt6{hc % 2}", f"wut6{hc % 2}"
        for kq in range(4):
            p.dma("pool",
                  wgt6.rearrange("p (k h) -> p k h", h=128)[:, 4 * kq:4 * kq + 4, :],
                  wgr1[:, 4 * kq:4 * kq + 4, hc * 128:(hc + 1) * 128],
                  "wgt_d", gkey)
            p.dma("pool",
                  wut6.rearrange("p (k h) -> p k h", h=128)[:, 4 * kq:4 * kq + 4, :],
                  wur1[:, 4 * kq:4 * kq + 4, hc * 128:(hc + 1) * 128],
                  "wut_d", ukey)
        for n0 in (0, 512):
            for k in range(NCH):
                p.mm(psG[:, :], wgt6[:, k * 128:(k + 1) * 128],
                     h4[:, k * TOK + n0:k * TOK + n0 + 512],
                     reads=(gkey, "h4"), writes="psG",
                     start=(k == 0), stop=(k == NCH - 1))
            for k in range(NCH):
                p.mm(psU[:, :], wut6[:, k * 128:(k + 1) * 128],
                     h4[:, k * TOK + n0:k * TOK + n0 + 512],
                     reads=(ukey, "h4"), writes="psU",
                     start=(k == 0), stop=(k == NCH - 1))
            p.act(sg6[:, n0:n0 + 512], psG[:, :], AF.Silu, reads="psG",
                  writes="sg6")
            p.act(su6[:, n0:n0 + 512], psU[:, :], AF.Copy, reads="psU",
                  writes="su6")
        p.tt(ght, sg6, su6, ALU.mult, reads=("sg6", "su6"), writes="ght")
        p.dma("sp", g1_s[hc * 128:(hc + 1) * 128, :], ght, "ght", "g1_s")
    p.phase()
    wdr1 = wdt_d.rearrange("(k p) d -> p k d", p=128)
    for th_ in range(2):
        gch = PB[:, 0:64 * 512]
        for k in range(64):
            p.dma("sp", gch[:, k * 512:(k + 1) * 512],
                  g1_s[k * 128:(k + 1) * 128, th_ * 512:(th_ + 1) * 512],
                  "g1_s", "gch6")
        for d in range(NCH):
            wdb = PB[:, 64 * 512:64 * 512 + HID]
            for kq in range(8):
                p.dma("pool",
                      wdb.rearrange("p (k m) -> p k m", m=128)[:, 8 * kq:8 * kq + 8, :],
                      wdr1[:, 8 * kq:8 * kq + 8, d * 128:(d + 1) * 128],
                      "wdt_d", "wdb6")
            for k in range(64):
                p.mm(psW[:, 0:512], wdb[:, k * 128:(k + 1) * 128],
                     gch[:, k * 512:(k + 1) * 512], reads=("wdb6", "gch6"),
                     writes="psW", start=(k == 0), stop=(k == 63))
            p.stt(xres[:, d * TOK + th_ * 512:d * TOK + th_ * 512 + 512],
                  psW[:, 0:512], bcols[:, 16 + d:17 + d],
                  xres[:, d * TOK + th_ * 512:d * TOK + th_ * 512 + 512],
                  ALU.mult, ALU.add, reads=("psW", "bcol6", "xres"),
                  writes="xres")
    for c in range(NCH):
        p.dma("sp", xo_d[c * 128:(c + 1) * 128, :],
              xres[:, c * TOK:(c + 1) * TOK], "xres", "xo")
    p.finish()
    return nc


def build_p1():
    nc = bass.Bass("TRN2", target_bir_lowering=False)
    MC0B = (6 * D - 2 * D) // NCORE
    MC1 = 6 * D // NCORE
    x_d = nc.dram_tensor("x", [D, TOK], F32, kind="ExternalInput").ap()
    c0_d = nc.dram_tensor("c0", [1, D], F32, kind="ExternalInput").ap()
    c1_d = nc.dram_tensor("c1", [1, D], F32, kind="ExternalInput").ap()
    wm0a_d = nc.dram_tensor("wm0a", [D, 2 * D], F32, kind="ExternalInput").ap()
    wm0b_d = nc.dram_tensor("wm0b", [D, MC0B], F32, kind="ExternalInput").ap()
    wm1_d = nc.dram_tensor("wm1", [D, MC1], F32, kind="ExternalInput").ap()
    win_d = nc.dram_tensor("win", [D, D], F32, kind="ExternalInput").ap()
    bin_d = nc.dram_tensor("binr", [1, D], F32, kind="ExternalInput").ap()
    wgx_d = nc.dram_tensor("wgx", [D, 128], F32, kind="ExternalInput").ap()
    wga_d = nc.dram_tensor("wga", [D, 128], F32, kind="ExternalInput").ap()
    bgx_d = nc.dram_tensor("bgx", [1, DH], F32, kind="ExternalInput").ap()
    bga_d = nc.dram_tensor("bga", [1, DH], F32, kind="ExternalInput").ap()
    ar_d = nc.dram_tensor("ar", [1, DH], F32, kind="ExternalInput").ap()
    ai_d = nc.dram_tensor("ai", [1, DH], F32, kind="ExternalInput").ap()
    summ_o = nc.dram_tensor("summ_o", [128, 40], F32, kind="ExternalOutput").ap()
    modx_o = nc.dram_tensor("modx_o", [1, MC0B + MC1], F32, kind="ExternalOutput").ap()
    st_ur = nc.dram_tensor("st_ur", [DH, TOK], F32, kind="ExternalOutput").ap()
    st_ui = nc.dram_tensor("st_ui", [DH, TOK], F32, kind="ExternalOutput").ap()
    st_ec = nc.dram_tensor("st_ec", [DH, TOK], F32, kind="ExternalOutput").ap()
    st_cs = nc.dram_tensor("st_cs", [DH, TOK], F32, kind="ExternalOutput").ap()
    st_sn = nc.dram_tensor("st_sn", [DH, TOK], F32, kind="ExternalOutput").ap()

    p = Prog(nc)
    ones = p.sb("ones", [128, 1], F32)
    onesrow = p.sb("onesrow", [1, 512], F32)
    p.memset(ones[:, :], 1.0, writes="ones")
    p.memset(onesrow[0:1, :], 1.0, writes="onesrow")
    PF = p.sb("PF", [128, 25600], F32)
    PB = p.sb("PB", [128, 45056], BF16)
    R = p.sb("R", [1, 4096], F32)
    bcols = p.sb("bcols", [128, 64], F32)
    psA = p.ps("psA", [1, 512], F32)
    psC = p.ps("psC", [128, 512], F32)
    stk01 = contextlib.ExitStack()
    psB = p.ps("psB01", [1, 512], F32, scope=stk01)
    rtc = p.ps("rtc01", [128, 16], F32, scope=stk01)
    psW = p.ps("psW01", [128, 1024], F32, scope=stk01)
    xres = PF[:, 0:16384]
    fscr = PF[:, 16384:25600]

    ct = R[0:1, 0:2 * D]
    p.dma("sp", ct[0:1, 0:D], c0_d, "c0", "ct")
    p.dma("sp", ct[0:1, D:2 * D], c1_d, "c1", "ct")
    p.act(ct, ct, AF.Silu, reads="ct", writes="ct")
    scols = bcols[:, 32:64]
    row_to_cols(p, ct[0:1, 0:D], "ct", scols, "scols", rtc, 16, ones)
    row_to_cols(p, ct[0:1, D:2 * D], "ct", scols, "scols", rtc, 16, ones,
                off=16)
    wmt = PF[:, 0:1024]
    mot = PF[0:1, 1024:1024 + 4096]
    motx = PF[0:1, 5120:5120 + MC0B + MC1]
    for n0 in range(0, 4096, 512):
        for k in range(NCH):
            wb = wmt[:, (k % 2) * 512:(k % 2) * 512 + 512]
            wkey = f"wmt{k % 2}"
            p.dma("sp", wb, wm0a_d[k * 128:(k + 1) * 128, n0:n0 + 512],
                  "wm0a", wkey)
            p.mm(psA[0:1, :], scols[:, k:k + 1], wb, reads=("scols", wkey),
                 writes="psA", start=(k == 0), stop=(k == NCH - 1))
        p.act(mot[0:1, n0:n0 + 512], psA[0:1, :], AF.Copy, reads="psA",
              writes="mot")
    for li, wsrc, mc in ((0, wm0b_d, MC0B), (1, wm1_d, MC1)):
        co = 0 if li == 0 else MC0B
        for n0 in range(0, mc, 512):
            n1 = min(n0 + 512, mc)
            for k in range(NCH):
                wb = wmt[:, (k % 2) * 512:(k % 2) * 512 + (n1 - n0)]
                wkey = f"wmt{k % 2}"
                p.dma("sp", wb, wsrc[k * 128:(k + 1) * 128, n0:n1],
                      f"wmx{li}", wkey)
                p.mm(psA[0:1, 0:n1 - n0], scols[:, li * 16 + k:li * 16 + k + 1],
                     wb, reads=("scols", wkey), writes="psA",
                     start=(k == 0), stop=(k == NCH - 1))
            p.act(motx[0:1, co + n0:co + n1], psA[0:1, 0:n1 - n0], AF.Copy,
                  reads="psA", writes="motx")
    p.dma("sp", modx_o, motx, "motx", "modx")
    p.phase()

    s = contextlib.ExitStack()
    h1 = PB[:, 0:16384]
    xh = PB[:, 16384:32768]
    wbuf = PB[:, 32768:36864]
    wg2 = PB[:, 36864:40960]
    sq = fscr[:, 0:512]
    tmp = fscr[:, 512:1024]
    srow = fscr[0:1, 1024:1024 + 2 * TOK]
    stat = fscr[0:1, 1024 + 2 * TOK:1024 + 5 * TOK]
    carow = fscr[0:1, 1024 + 5 * TOK:1024 + 5 * TOK + D]
    var = fscr[0:1, 1024 + 5 * TOK + D:1024 + 6 * TOK + D]
    xa = PF[:, 9216:11264]
    p.ts(carow, mot[0:1, D:2 * D], 1.0, None, ALU.add, ALU.bypass,
         reads="mot", writes="ln1")
    p.memset(stat[0:1, 2 * TOK:3 * TOK], 1.0, writes="ln1")
    for n0 in range(0, TOK, 512):
        for c in range(NCH):
            xb = xa[:, (c % 2) * 1024:(c % 2) * 1024 + 512]
            xkey = f"xa{c % 2}"
            p.dma("sp", xb, x_d[c * 128:(c + 1) * 128, n0:n0 + 512], "x_d", xkey)
            p.mm(psA[0:1, :], ones[:, 0:1], xb, reads=(xkey, "ones"),
                 writes="psA", start=(c == 0), stop=(c == NCH - 1))
            p.act(sq, xb, AF.Square, reads=xkey, writes="sq1")
            p.mm(psB[0:1, :], ones[:, 0:1], sq, reads=("sq1", "ones"),
                 writes="psB", start=(c == 0), stop=(c == NCH - 1))
        p.act(srow[0:1, n0:n0 + 512], psA[0:1, :], AF.Copy, reads="psA",
              writes="ln1", scale=1.0 / D)
        p.act(srow[0:1, TOK + n0:TOK + n0 + 512], psB[0:1, :], AF.Copy,
              reads="psB", writes="ln1", scale=1.0 / D)
    mean = srow[0:1, 0:TOK]
    msq = srow[0:1, TOK:2 * TOK]
    p.tt(var, mean, mean, ALU.mult, reads="ln1", writes="ln1v")
    p.tt(var, msq, var, ALU.subtract, reads=("ln1", "ln1v"), writes="ln1v")
    p.act(var, var, AF.Ln, reads="ln1v", writes="ln1v", bias=1e-5)
    p.act(stat[0:1, 0:TOK], var, AF.Exp, reads="ln1v", writes="ln1", scale=-0.5)
    p.tt(stat[0:1, TOK:2 * TOK], mean, stat[0:1, 0:TOK], ALU.mult,
         reads="ln1", writes="ln1")
    p.ts(stat[0:1, TOK:2 * TOK], stat[0:1, TOK:2 * TOK], -1.0, None, ALU.mult,
         ALU.bypass, reads="ln1", writes="ln1")
    for c in range(NCH):
        xb = xa[:, (c % 2) * 1024:(c % 2) * 1024 + 1024]
        xkey = f"xa{c % 2}"
        p.dma("sp", xb, x_d[c * 128:(c + 1) * 128, :], "x_d", xkey)
        for n0 in (0, 512):
            p.mm(psC[:, :], carow[0:1, c * 128:(c + 1) * 128],
                 stat[0:1, n0:n0 + 512], reads=("ln1",), writes="psC")
            p.tt(tmp, xb[:, n0:n0 + 512], psC[:, :], ALU.mult,
                 reads=(xkey, "psC"), writes="ln1t")
            p.mm(psC[:, :], carow[0:1, c * 128:(c + 1) * 128],
                 stat[0:1, TOK + n0:TOK + n0 + 512], reads=("ln1",),
                 writes="psC", start=True, stop=False)
            p.mm(psC[:, :], mot[0:1, c * 128:(c + 1) * 128],
                 stat[0:1, 2 * TOK + n0:2 * TOK + n0 + 512],
                 reads=("mot", "ln1"), writes="psC", start=False, stop=True)
            p.tt(h1[:, c * TOK + n0:c * TOK + n0 + 512], tmp, psC[:, :],
                 ALU.add, reads=("ln1t", "psC"), writes="h1")
    winr = win_d.rearrange("(k p) m -> p k m", p=128)
    brow = fscr[0:1, 0:D]
    p.phase()
    p.dma("sp", brow, bin_d, "bin_d", "brow")
    for m in range(NCH):
        wb = wbuf[:, (m % 2) * D:(m % 2) * D + D]
        wkey = f"wbuf{m % 2}"
        for kq in range(4):
            p.dma("pool",
                  wb.rearrange("p (k m) -> p k m", k=NCH)[:, 4 * kq:4 * kq + 4, :],
                  winr[:, 4 * kq:4 * kq + 4, m * 128:(m + 1) * 128],
                  "win_d", wkey)
        for n0 in (0, 512):
            p.mm(psW[:, n0:n0 + 512], brow[0:1, m * 128:(m + 1) * 128],
                 onesrow[0:1, :], reads=("brow", "onesrow"), writes="psW",
                 start=True, stop=False)
            for k in range(NCH):
                p.mm(psW[:, n0:n0 + 512], wb[:, k * 128:(k + 1) * 128],
                     h1[:, k * TOK + n0:k * TOK + n0 + 512],
                     reads=(wkey, "h1"), writes="psW",
                     start=False, stop=(k == NCH - 1))
        p.act(xh[:, m * TOK:(m + 1) * TOK], psW[:, :], AF.Copy,
              reads="psW", writes="xh")
    p.phase()
    for k in range(NCH):
        p.dma("pool", wg2[:, k * 128:(k + 1) * 128],
              wgx_d[k * 128:(k + 1) * 128, :], "wgx_d", "wg2")
        p.dma("pool", wg2[:, 2048 + k * 128:2048 + (k + 1) * 128],
              wga_d[k * 128:(k + 1) * 128, :], "wga_d", "wg2")
    rows2 = R[0:1, 0:4096]
    p.dma("sp", rows2[0:1, 0:DH], bgx_d, "bgx_d", "R2")
    p.dma("sp", rows2[0:1, DH:2 * DH], bga_d, "bga_d", "R2")
    p.dma("sp", rows2[0:1, 3 * DH:4 * DH], ai_d, "ai_d", "R2")
    spr = fscr[0:1, 0:2 * DH]
    p.dma("sp", spr[0:1, 0:DH], ar_d, "ar_d", "spr")
    p.act(spr[0:1, DH:2 * DH], spr[0:1, 0:DH], AF.Abs, reads="spr", writes="spr")
    p.act(spr[0:1, DH:2 * DH], spr[0:1, DH:2 * DH], AF.Exp, reads="spr",
          writes="spr", scale=-1.0)
    p.act(spr[0:1, DH:2 * DH], spr[0:1, DH:2 * DH], AF.Ln, reads="spr",
          writes="spr", bias=1.0)
    p.act(spr[0:1, 0:DH], spr[0:1, 0:DH], AF.Relu, reads="spr", writes="spr")
    p.tt(rows2[0:1, 2 * DH:3 * DH], spr[0:1, DH:2 * DH], spr[0:1, 0:DH],
         ALU.add, reads="spr", writes="R2")
    row_to_cols(p, rows2[0:1, 0:DH], "R2", bcols, "bcolg", rtc, 8, ones, off=0)
    row_to_cols(p, rows2[0:1, DH:2 * DH], "R2", bcols, "bcolg", rtc, 8, ones,
                off=8)
    row_to_cols(p, rows2[0:1, 2 * DH:3 * DH], "R2", bcols, "bcolg", rtc, 8,
                ones, off=16, scale=-8.0)
    row_to_cols(p, rows2[0:1, 3 * DH:4 * DH], "R2", bcols, "bcolg", rtc, 8,
                ones, off=24)
    p.phase()
    gx = PF[:, 0:1024]
    ga = PF[:, 5120:6144]
    la = PF[:, 6144:7168]
    th = PF[:, 7168:8192]
    sn = PF[:, 8192:9216]
    cs = PF[:, 9216:10240]
    t1 = PF[:, 10240:11264]
    t2 = PF[:, 11264:12288]
    ur = PF[:, 12288:13312]
    ui = PF[:, 13312:14336]
    eC = PF[:, 14336:15360]
    ost = PF[:, 15360:16384]
    summ = PF[:, 1024:1064]
    p.memset(ost[:, :], 1.0, writes="ost")
    for b in range(8):
        for gi, (gt_, gkey, boff) in enumerate(((gx, "gx", 0), (ga, "gag", 8))):
            woff = gi * 2048
            for n0 in (0, 512):
                for kk in (2 * b, 2 * b + 1):
                    p.mm(psW[:, n0:n0 + 512],
                         wg2[:, woff + kk * 128:woff + (kk + 1) * 128],
                         xh[:, kk * TOK + n0:kk * TOK + n0 + 512],
                         reads=("wg2", "xh"), writes="psW",
                         start=(kk == 2 * b), stop=(kk == 2 * b + 1))
            p.act(gt_[:, :], psW[:, :], AF.Sigmoid, reads=("psW", "bcolg"),
                  writes=gkey, bias=bcols[:, boff + b:boff + b + 1])
        p.ts(la[:, :], ga, bcols[:, 16 + b:17 + b], None, ALU.mult,
             ALU.bypass, reads=("gag", "bcolg"), writes="la")
        p.ts(th[:, :], ga, bcols[:, 24 + b:25 + b], None, ALU.mult,
             ALU.bypass, reads=("gag", "bcolg"), writes="th")
        p.act(t1[:, :], la, AF.Exp, reads="la", writes="t1", scale=2.0)
        p.act(t1[:, :], t1, AF.Ln, reads="t1", writes="t1", scale=-1.0,
              bias=1.0)
        p.act(t1[:, :], t1, AF.Exp, reads="t1", writes="t1", scale=0.5)
        p.tt(gx[:, :], gx, t1, ALU.mult, reads=("gx", "t1"), writes="gx")
        p.tt(t1[:, :], gx, xh[:, b * TOK:(b + 1) * TOK], ALU.mult,
             reads=("gx", "xh"), writes="t1")
        p.tt(t2[:, :], gx, xh[:, (8 + b) * TOK:(9 + b) * TOK], ALU.mult,
             reads=("gx", "xh"), writes="t2")
        p.scan(th[:, :], ost, th[:, :], 0.0, reads=("ost", "th"), writes="th")
        p.scan(eC[:, :], ost, la[:, :], 0.0, reads=("ost", "la"), writes="eC")
        p.act(la[:, :], la, AF.Exp, reads="la", writes="la")
        p.act(eC[:, :], eC, AF.Exp, reads="eC", writes="eC")
        p.ts(cs[:, :], th, INV2PI, None, ALU.mult, ALU.bypass, reads="th",
             writes="cs")
        p.ts(sn[:, :], cs, MAGIC, MAGIC, ALU.add, ALU.subtract, reads="cs",
             writes="sn")
        p.tt(cs[:, :], cs, sn, ALU.subtract, reads=("cs", "sn"), writes="cs")
        p.act(sn[:, :], cs, AF.Sin, reads="cs", writes="sn", scale=TWOPI)
        p.ts(cs[:, :], th, INV2PI, 0.25, ALU.mult, ALU.add, reads="th",
             writes="cs")
        p.ts(th[:, :], cs, MAGIC, MAGIC, ALU.add, ALU.subtract, reads="cs",
             writes="th")
        p.tt(cs[:, :], cs, th, ALU.subtract, reads=("cs", "th"), writes="cs")
        p.act(cs[:, :], cs, AF.Sin, reads="cs", writes="cs", scale=TWOPI)
        p.tt(ur[:, :], sn, t2, ALU.mult, reads=("sn", "t2"), writes="ur")
        p.tt(ui[:, :], sn, t1, ALU.mult, reads=("sn", "t1"), writes="ui")
        p.tt(t1[:, :], cs, t1, ALU.mult, reads=("cs", "t1"), writes="t1")
        p.tt(t2[:, :], cs, t2, ALU.mult, reads=("cs", "t2"), writes="t2")
        p.tt(t1[:, :], t1, ur, ALU.add, reads=("t1", "ur"), writes="t1")
        p.tt(t2[:, :], t2, ui, ALU.subtract, reads=("t2", "ui"), writes="t2")
        p.scan(ur[:, :], la, t1, 0.0, reads=("la", "t1"), writes="ur")
        p.scan(ui[:, :], la, t2, 0.0, reads=("la", "t2"), writes="ui")
        p.dma("sp", st_ur[b * 128:(b + 1) * 128, :], ur, "ur", "st")
        p.dma("sp", st_ui[b * 128:(b + 1) * 128, :], ui, "ui", "st")
        p.dma("sp", st_ec[b * 128:(b + 1) * 128, :], eC, "eC", "st")
        p.dma("sp", st_cs[b * 128:(b + 1) * 128, :], cs, "cs", "st")
        p.dma("sp", st_sn[b * 128:(b + 1) * 128, :], sn, "sn", "st")
        for qi, tile in enumerate((eC, cs, sn, ur, ui)):
            p.copy(summ[:, qi * 8 + b:qi * 8 + b + 1], tile[:, TOK - 1:TOK],
                   reads={0: "eC", 1: "cs", 2: "sn", 3: "ur", 4: "ui"}[qi],
                   writes="summ")
    p.dma("sp", summ_o, summ[:, :], "summ", "summo")
    p.finish()
    return nc



def kernel_fused(**inp):
    f32 = np.float32
    MC = 6 * D // NCORE
    nc = build_fused()
    x_fm = _cc(inp["L0_attn_x"][0].astype(f32).T)
    wgx2 = _cc(inp["L0_attn_Wg_x"].reshape(D, 128))
    wga2 = _cc(inp["L0_attn_Wg_a"].reshape(D, 128))
    ind4 = np.zeros((4, 128), f32)
    for j in range(4):
        ind4[j, j * 32:(j + 1) * 32] = 1.0
    idbf = np.eye(128, dtype=np.float32).astype(BF)
    wgt = _cc(inp["L1_ffn_w_gate"].T)
    wut = _cc(inp["L1_ffn_w_up"].T)
    wdt = _cc(inp["L1_ffn_w_down"].T)
    im = []
    for r in range(NCORE):
        ts_ = slice(r * TOK, (r + 1) * TOK)
        alphac = np.zeros((128, 7), f32)
        alphac[:, 0:r] = 1.0
        im.append({
            "x": _cc(x_fm[:, ts_]),
            "c0": inp["L0_c"], "c1": inp["L1_c"],
            "wm0": _cc(inp["L0_wmod"][:, r * MC:(r + 1) * MC]),
            "wm1": _cc(inp["L1_wmod"][:, r * MC:(r + 1) * MC]),
            "alphac": alphac, "ind4": ind4, "idbf": idbf,
            "win": inp["L0_attn_W_in"], "binr": inp["L0_attn_b_in"][None, :],
            "wgx": wgx2, "wga": wga2,
            "bgx": inp["L0_attn_bg_x"][None, :],
            "bga": inp["L0_attn_bg_a"][None, :],
            "ar": inp["L0_attn_a_real_param"][None, :],
            "ai": inp["L0_attn_a_imag_param"][None, :],
            "wout": inp["L0_attn_W_out"], "boutr": inp["L0_attn_b_out"][None, :],
            "wg": inp["L0_ffn_w_gate"], "wu": inp["L0_ffn_w_up"],
            "wd": inp["L0_ffn_w_down"],
            "cq": inp["L1_attn_wq_codes"], "sq": inp["L1_attn_wq_scales"],
            "bq": inp["L1_attn_bq"][None, :],
            "ck": inp["L1_attn_wk_codes"], "sk": inp["L1_attn_wk_scales"],
            "bk": inp["L1_attn_bk"][None, :],
            "cv": inp["L1_attn_wv_codes"], "sv": inp["L1_attn_wv_scales"],
            "bv": inp["L1_attn_bv"][None, :],
            "co": inp["L1_attn_wo_codes"], "so": inp["L1_attn_wo_scales"],
            "bo": inp["L1_attn_bo"][None, :],
            "wgt": wgt, "wut": wut, "wdt": wdt,
        })
    res = run_bass_kernel_spmd(nc, im, core_ids=list(range(NCORE)))
    xo = np.concatenate([res.results[r]["xo"] for r in range(NCORE)], axis=1)
    return _cc(xo.T)[None].astype(np.float32)


def kernel(**inp):
    return kernel_3l(**inp)


def build_p2():
    nc = bass.Bass("TRN2", target_bir_lowering=False)
    x_d = nc.dram_tensor("x", [D, TOK], F32, kind="ExternalInput").ap()
    carg = nc.dram_tensor("carg", [NCORE * 128, 40], F32, kind="ExternalInput").ap()
    alpha_d = nc.dram_tensor("alphac", [128, 7], F32, kind="ExternalInput").ap()
    ind4_d = nc.dram_tensor("ind4", [4, 128], F32, kind="ExternalInput").ap()
    st_ur = nc.dram_tensor("st_ur", [DH, TOK], F32, kind="ExternalInput").ap()
    st_ui = nc.dram_tensor("st_ui", [DH, TOK], F32, kind="ExternalInput").ap()
    st_ec = nc.dram_tensor("st_ec", [DH, TOK], F32, kind="ExternalInput").ap()
    st_cs = nc.dram_tensor("st_cs", [DH, TOK], F32, kind="ExternalInput").ap()
    st_sn = nc.dram_tensor("st_sn", [DH, TOK], F32, kind="ExternalInput").ap()
    modr0 = nc.dram_tensor("modr0", [1, 4 * D], F32, kind="ExternalInput").ap()
    mods1 = nc.dram_tensor("mods1", [1, 2 * D], F32, kind="ExternalInput").ap()
    wout_d = nc.dram_tensor("wout", [D, D], F32, kind="ExternalInput").ap()
    bout_d = nc.dram_tensor("boutr", [1, D], F32, kind="ExternalInput").ap()
    wg_d = nc.dram_tensor("wg", [D, HID], F32, kind="ExternalInput").ap()
    wu_d = nc.dram_tensor("wu", [D, HID], F32, kind="ExternalInput").ap()
    wd_d = nc.dram_tensor("wd", [HID, D], F32, kind="ExternalInput").ap()
    cW, sW, bW = {}, {}, {}
    for w in ("q", "k", "v"):
        cW[w] = nc.dram_tensor(f"c{w}", [D, D], F32, kind="ExternalInput").ap()
        sW[w] = nc.dram_tensor(f"s{w}", [64, D], F32, kind="ExternalInput").ap()
        bW[w] = nc.dram_tensor(f"b{w}", [1, D], F32, kind="ExternalInput").ap()
    x2_o = nc.dram_tensor("x2_o", [D, TOK], F32, kind="ExternalOutput").ap()
    q_o = nc.dram_tensor("q_o", [D, TOK], BF16, kind="ExternalOutput").ap()
    kv_o = nc.dram_tensor("kv_o", [2 * D, TOK], BF16, kind="ExternalOutput").ap()
    n_o = nc.dram_tensor("n_o", [1, 32], F32, kind="ExternalOutput").ap()
    hh_s = nc.dram_tensor("hh_s", [TOK, HID], F32, kind="Internal").ap()
    g_s = nc.dram_tensor("g_s", [TOK, HID], BF16, kind="Internal").ap()

    p = Prog(nc)
    ones = p.sb("ones", [128, 1], F32)
    onesrow = p.sb("onesrow", [1, 512], F32)
    p.memset(ones[:, :], 1.0, writes="ones")
    p.memset(onesrow[0:1, :], 1.0, writes="onesrow")
    PF = p.sb("PF", [128, 23040], F32)
    PB = p.sb("PB", [128, 49152], BF16)
    R = p.sb("R", [1, 4096], F32)
    bcols = p.sb("bcols", [128, 64], F32)
    ind4 = p.sb("ind4t", [4, 128], F32)
    alph = p.sb("alph", [128, 7], F32)
    psA = p.ps("psA", [1, 512], F32)
    psC = p.ps("psC", [128, 512], F32)
    stk01 = contextlib.ExitStack()
    xres = PF[:, 0:16384]
    fscr = PF[:, 16384:23040]
    p.dma("sp", ind4[0:4, :], ind4_d, "ind4d", "ind4t")
    p.dma("sp", alph[:, :], alpha_d, "alpha_d", "alph")
    Hr = PF[:, 0:8]
    Hi = PF[:, 8:16]
    jb = PF[:, 16:56]
    tr_ = PF[:, 56:64]
    ti_ = PF[:, 64:72]
    t3_ = PF[:, 72:80]
    t4_ = PF[:, 80:88]
    p.memset(Hr, 0.0, writes="Hr")
    p.memset(Hi, 0.0, writes="Hi")
    for j in range(7):
        p.dma("sp", jb, carg[j * 128:(j + 1) * 128, :], "carg", "jb")
        Me, csj, snj, urj, uij = (jb[:, 0:8], jb[:, 8:16], jb[:, 16:24],
                                  jb[:, 24:32], jb[:, 32:40])
        p.tt(tr_, Me, Hr, ALU.mult, reads=("jb", "Hr"), writes="tr")
        p.tt(tr_, tr_, urj, ALU.add, reads=("tr", "jb"), writes="tr")
        p.tt(ti_, Me, Hi, ALU.mult, reads=("jb", "Hi"), writes="ti")
        p.tt(ti_, ti_, uij, ALU.add, reads=("ti", "jb"), writes="ti")
        p.tt(t3_, csj, tr_, ALU.mult, reads=("jb", "tr"), writes="t3")
        p.tt(t4_, snj, ti_, ALU.mult, reads=("jb", "ti"), writes="t4")
        p.tt(t3_, t3_, t4_, ALU.subtract, reads=("t3", "t4"), writes="t3")
        p.tt(t3_, t3_, Hr, ALU.subtract, reads=("t3", "Hr"), writes="t3")
        p.stt(Hr, t3_, alph[:, j:j + 1], Hr, ALU.mult, ALU.add,
              reads=("t3", "alph", "Hr"), writes="Hr")
        p.tt(t3_, snj, tr_, ALU.mult, reads=("jb", "tr"), writes="t3")
        p.tt(tr_, csj, ti_, ALU.mult, reads=("jb", "ti"), writes="tr")
        p.tt(t3_, t3_, tr_, ALU.add, reads=("t3", "tr"), writes="t3")
        p.tt(t3_, t3_, Hi, ALU.subtract, reads=("t3", "Hi"), writes="t3")
        p.stt(Hi, t3_, alph[:, j:j + 1], Hi, ALU.mult, ALU.add,
              reads=("t3", "alph", "Hi"), writes="Hi")
    urT = PF[:, 1024:2048]
    uiT = PF[:, 2048:3072]
    ecT = PF[:, 3072:4096]
    csT = PF[:, 4096:5120]
    snT = PF[:, 5120:6144]
    w1 = PF[:, 6144:7168]
    w2 = PF[:, 7168:8192]
    hfm = PB[:, 0:16384]
    for b in range(8):
        p.dma("sp", urT, st_ur[b * 128:(b + 1) * 128, :], "st", "urT")
        p.dma("sp", uiT, st_ui[b * 128:(b + 1) * 128, :], "st", "uiT")
        p.dma("sp", ecT, st_ec[b * 128:(b + 1) * 128, :], "st", "ecT")
        p.dma("sp", csT, st_cs[b * 128:(b + 1) * 128, :], "st", "csT")
        p.dma("sp", snT, st_sn[b * 128:(b + 1) * 128, :], "st", "snT")
        p.stt(urT, ecT, Hr[:, b:b + 1], urT, ALU.mult, ALU.add,
              reads=("ecT", "Hr", "urT"), writes="urT")
        p.stt(uiT, ecT, Hi[:, b:b + 1], uiT, ALU.mult, ALU.add,
              reads=("ecT", "Hi", "uiT"), writes="uiT")
        p.tt(w1, csT, urT, ALU.mult, reads=("csT", "urT"), writes="w1")
        p.tt(w2, snT, uiT, ALU.mult, reads=("snT", "uiT"), writes="w2")
        p.tt(hfm[:, b * TOK:(b + 1) * TOK], w1, w2, ALU.subtract,
             reads=("w1", "w2"), writes="hfm")
        p.tt(w1, snT, urT, ALU.mult, reads=("snT", "urT"), writes="w1")
        p.tt(w2, csT, uiT, ALU.mult, reads=("csT", "uiT"), writes="w2")
        p.tt(hfm[:, (8 + b) * TOK:(9 + b) * TOK], w1, w2, ALU.add,
             reads=("w1", "w2"), writes="hfm")
    p.phase()

    stk01.close()
    stk3 = contextlib.ExitStack()
    psB = p.ps("psB3", [1, 512], F32, scope=stk3)
    rtc = p.ps("rtc3", [128, 16], F32, scope=stk3)
    psW = p.ps("psW3", [128, 1024], F32, scope=stk3)
    psG = p.ps("psG3", [128, 512], F32, scope=stk3)
    psU = p.ps("psU3", [128, 512], F32, scope=stk3)
    grow = R[0:1, 0:2 * D]
    t1r = fscr[0:1, 0:D]
    t2r = fscr[0:1, D:2 * D]
    p.dma("sp", grow[0:1, 0:D], modr0[0:1, 0:D], "m0g", "grow")
    p.dma("sp", grow[0:1, D:2 * D], modr0[0:1, 3 * D:4 * D], "m0g", "grow")
    for gi in range(2):
        gseg = grow[0:1, gi * D:(gi + 1) * D]
        p.act(t1r, gseg, AF.Sign, reads="grow", writes="t1r")
        p.act(t2r, gseg, AF.Abs, reads="grow", writes="t2r")
        p.ts(t2r, t2r, 0.25, None, ALU.max, ALU.bypass, reads="t2r",
             writes="t2r")
        p.tt(gseg, t1r, t2r, ALU.mult, reads=("t1r", "t2r"), writes="grow")
        row_to_cols(p, gseg, "grow", bcols, "bcolr", rtc, 16, ones,
                    off=gi * 16)
    for c in range(NCH):
        p.dma("sp", xres[:, c * TOK:(c + 1) * TOK],
              x_d[c * 128:(c + 1) * 128, :], "x_d", "xres")
    brow = fscr[0:1, 0:D]
    p.dma("sp", brow, bout_d, "bout", "brow3")
    woutr = wout_d.rearrange("(k p) m -> p k m", p=128)
    yt = fscr[:, 2048:3072]
    wbuf3 = PB[:, 16384:20480]
    for m in range(NCH):
        wb = wbuf3[:, (m % 2) * D:(m % 2) * D + D]
        wkey = f"wb3{m % 2}"
        for kq in range(4):
            p.dma("pool",
                  wb.rearrange("p (k m) -> p k m", k=NCH)[:, 4 * kq:4 * kq + 4, :],
                  woutr[:, 4 * kq:4 * kq + 4, m * 128:(m + 1) * 128],
                  "wout_d", wkey)
        for n0 in (0, 512):
            half = (n0 // 512) % 2
            pw = psW[:, half * 512:half * 512 + 512]
            pwk = "psWa" if half == 0 else "psWb"
            p.mm(pw, brow[0:1, m * 128:(m + 1) * 128],
                 onesrow[0:1, :], reads=("brow3", "onesrow"), writes=pwk,
                 start=True, stop=False)
            for k in range(NCH):
                p.mm(pw, wb[:, k * 128:(k + 1) * 128],
                     hfm[:, k * TOK + n0:k * TOK + n0 + 512],
                     reads=(wkey, "hfm"), writes=pwk,
                     start=False, stop=(k == NCH - 1))
            p.act(yt[:, n0:n0 + 512], pw, AF.Copy, reads=pwk, writes="yt")
        p.stt(xres[:, m * TOK:(m + 1) * TOK], yt, bcols[:, m:m + 1],
              xres[:, m * TOK:(m + 1) * TOK], ALU.mult, ALU.add,
              reads=("yt", "bcolr", "xres"), writes="xres")
    p.phase()
    p.dma("sp", R[0:1, 0:2 * D], modr0[0:1, D:3 * D], "m0g", "Rln2")
    h2 = PB[:, 0:16384]
    emit_ln_affine(p, xres, "xres", h2, "h2", R, "Rln2", 0, ones, "f2",
                  TOK, psA, psB, psC, fscr, onesrow=onesrow)
    p.phase()
    wgr = wg_d.rearrange("(k p) h -> p k h", p=128)
    wur = wu_d.rearrange("(k p) h -> p k h", p=128)
    sg = fscr[:, 0:512]
    su = fscr[:, 512:1024]
    shh = fscr[:, 1024:1536]
    sg2 = fscr[:, 1536:2048]
    su2 = fscr[:, 2048:2560]
    shh2 = fscr[:, 2560:3072]
    for ht in range(16):
        hb = ht % 2
        wgt3 = PB[:, 16384 + hb * 16384:16384 + hb * 16384 + 8192]
        wut3 = PB[:, 24576 + hb * 16384:24576 + hb * 16384 + 8192]
        gk, uk = f"wgt3{hb}", f"wut3{hb}"
        for kq in range(4):
            p.dma("pool",
                  wgt3.rearrange("p (k h) -> p k h", h=512)[:, 4 * kq:4 * kq + 4, :],
                  wgr[:, 4 * kq:4 * kq + 4, ht * 512:(ht + 1) * 512],
                  "wg_d", gk)
            p.dma("pool",
                  wut3.rearrange("p (k h) -> p k h", h=512)[:, 4 * kq:4 * kq + 4, :],
                  wur[:, 4 * kq:4 * kq + 4, ht * 512:(ht + 1) * 512],
                  "wu_d", uk)
        for tt in range(8):
            par = tt % 2
            pg = psG[:, :] if par == 0 else psW[:, 0:512]
            pu = psU[:, :] if par == 0 else psW[:, 512:1024]
            pgk = "psG" if par == 0 else "psWa"
            puk = "psU" if par == 0 else "psWb"
            sgx = sg if par == 0 else sg2
            sux = su if par == 0 else su2
            shx = shh if par == 0 else shh2
            sgk, suk, shk = (("sg", "su", "shh") if par == 0 else
                             ("sg2", "su2", "shh2"))
            for k in range(NCH):
                p.mm(pg, h2[:, k * TOK + tt * 128:k * TOK + (tt + 1) * 128],
                     wgt3[:, k * 512:(k + 1) * 512], reads=("h2", gk),
                     writes=pgk, start=(k == 0), stop=(k == NCH - 1))
            for k in range(NCH):
                p.mm(pu, h2[:, k * TOK + tt * 128:k * TOK + (tt + 1) * 128],
                     wut3[:, k * 512:(k + 1) * 512], reads=("h2", uk),
                     writes=puk, start=(k == 0), stop=(k == NCH - 1))
            p.act(sgx, pg, AF.Silu, reads=pgk, writes=sgk)
            p.act(sux, pu, AF.Copy, reads=puk, writes=suk)
            p.tt(shx, sgx, sux, ALU.mult, reads=(sgk, suk), writes=shk)
            p.dma("sp", hh_s[tt * 128:(tt + 1) * 128, ht * 512:(ht + 1) * 512],
                  shx, shk, "hh_s")
    p.phase()
    SEC = 2048
    hh = fscr[:, 0:SEC]
    ha = fscr[:, SEC:2 * SEC]
    s1 = fscr[:, 2 * SEC:2 * SEC + SEC // 4]
    s2 = fscr[:, 2 * SEC + SEC // 4:2 * SEC + SEC // 2]
    tq = PB[:, 16384:16384 + 6 * (SEC // 4)]
    gt = PB[:, 16384 + 6 * (SEC // 4):16384 + 6 * (SEC // 4) + SEC]
    Q = SEC // 4
    for tt in range(8):
        for sec in range(HID // SEC):
            p.dma("sp", hh, hh_s[tt * 128:(tt + 1) * 128,
                                  sec * SEC:(sec + 1) * SEC], "hh_s", "hh")
            p.act(ha, hh, AF.Abs, reads="hh", writes="ha")
            hq = [ha.rearrange("p (g f) -> p g f", f=4)[:, :, i] for i in range(4)]
            pairs = [(0, 1), (0, 2), (0, 3), (1, 2), (1, 3), (2, 3)]
            for pi, (a, b) in enumerate(pairs):
                p.tt(tq[:, pi * Q:(pi + 1) * Q], hq[a], hq[b], ALU.is_ge,
                     reads="ha", writes="tq")
            t01, t02, t03, t12, t13, t23 = [tq[:, i * Q:(i + 1) * Q]
                                            for i in range(6)]
            hhq = [hh.rearrange("p (g f) -> p g f", f=4)[:, :, i] for i in range(4)]
            gq = [gt.rearrange("p (g f) -> p g f", f=4)[:, :, i] for i in range(4)]
            p.tt(s1, t01, t02, ALU.add, reads="tq", writes="s1")
            p.tt(s1, s1, t03, ALU.add, reads=("s1", "tq"), writes="s1")
            p.stt(gq[0], s1, 1.5, hhq[0], ALU.is_ge, ALU.mult,
                  reads=("s1", "hh"), writes="gt")
            p.tt(s2, t12, t13, ALU.add, reads="tq", writes="s2")
            p.tt(s2, s2, t01, ALU.subtract, reads=("s2", "tq"), writes="s2")
            p.stt(gq[1], s2, 0.5, hhq[1], ALU.is_ge, ALU.mult,
                  reads=("s2", "hh"), writes="gt")
            p.tt(s1, t02, t12, ALU.add, reads="tq", writes="s1")
            p.tt(s1, s1, t23, ALU.subtract, reads=("s1", "tq"), writes="s1")
            p.stt(gq[2], s1, 0.5, hhq[2], ALU.is_le, ALU.mult,
                  reads=("s1", "hh"), writes="gt")
            p.tt(s2, t03, t13, ALU.add, reads="tq", writes="s2")
            p.tt(s2, s2, t23, ALU.add, reads=("s2", "tq"), writes="s2")
            p.stt(gq[3], s2, 1.5, hhq[3], ALU.is_le, ALU.mult,
                  reads=("s2", "hh"), writes="gt")
            p.dma("sp", g_s[tt * 128:(tt + 1) * 128, sec * SEC:(sec + 1) * SEC],
                  gt, "gt", "g_s")
    p.phase()
    wdr = wd_d.rearrange("(k p) d -> p k d", p=128)
    for th_ in range(2):
        gch = PB[:, 0:64 * 512]
        for k in range(64):
            p.op("sp", lambda E, dst=gch[:, k * 512:(k + 1) * 512],
                 src=g_s[th_ * 512:(th_ + 1) * 512, k * 128:(k + 1) * 128]:
                 E.dma_start(out=dst, in_=src, transpose=True),
                 reads=("g_s",), writes=("gch",), dma=True)
        for d in range(NCH):
            db = d % 2
            wdb = PB[:, 64 * 512 + db * HID:64 * 512 + db * HID + HID]
            wkey2 = f"wdb{db}"
            for kq in range(8):
                p.dma("pool",
                      wdb.rearrange("p (k m) -> p k m", m=128)[:, 8 * kq:8 * kq + 8, :],
                      wdr[:, 8 * kq:8 * kq + 8, d * 128:(d + 1) * 128],
                      "wd_d", wkey2)
            pw = psW[:, db * 512:db * 512 + 512]
            pwk = "psWa" if db == 0 else "psWb"
            for k in range(64):
                p.mm(pw, wdb[:, k * 128:(k + 1) * 128],
                     gch[:, k * 512:(k + 1) * 512], reads=(wkey2, "gch"),
                     writes=pwk, start=(k == 0), stop=(k == 63))
            p.stt(xres[:, d * TOK + th_ * 512:d * TOK + th_ * 512 + 512],
                  pw, bcols[:, 16 + d:17 + d],
                  xres[:, d * TOK + th_ * 512:d * TOK + th_ * 512 + 512],
                  ALU.mult, ALU.add, reads=(pwk, "bcolr", "xres"),
                  writes="xres")
    p.phase()

    stk3.close()
    stk4 = contextlib.ExitStack()
    psB = p.ps("psB4", [1, 512], F32, scope=stk4)
    psW = p.ps("psW4", [128, 1024], F32, scope=stk4)
    p.dma("sp", R[0:1, 0:2 * D], mods1[0:1, 0:2 * D], "m1g", "Rln3")
    h3 = PB[:, 0:16384]
    emit_ln_affine(p, xres, "xres", h3, "h3", R, "Rln3", 0, ones, "f3",
                  TOK, psA, psB, psC, fscr, onesrow=onesrow)
    p.phase()
    qres = PB[:, 16384:32768]
    sc4 = fscr[0:4, 0:2048]
    nrow = fscr[0:1, 2048:2080]
    brow4 = fscr[0:1, 2560:2560 + D]
    sqv = fscr[:, 4608:5632]
    nr2 = fscr[0:1, 5632:6656]
    qst = PB[:, 40960:40960 + 1024]
    for wi, w in enumerate(("q", "k", "v")):
        codes_r = cW[w].rearrange("(k p) m -> p k m", p=128)
        scales_r = sW[w].rearrange("(k j) m -> j k m", j=4)
        p.dma("sp", brow4, bW[w], f"b{w}", "brow4")
        for m in range(NCH):
            p.dma("sp", sc4[0:4, 0:2048].rearrange("j (k m) -> j k m", m=128),
                  scales_r[:, :, m * 128:(m + 1) * 128], f"s{w}", "sc4")
            cod = PB[:, 32768 + (m % 2) * 2048:32768 + (m % 2) * 2048 + 2048]
            ckey = f"cod{m % 2}"
            wm = PB[:, 36864 + (m % 2) * 2048:36864 + (m % 2) * 2048 + 2048]
            wmkey = f"wm{m % 2}"
            for kq in range(4):
                p.dma("pool",
                      cod.rearrange("p (k m) -> p k m", m=128)[:, 4 * kq:4 * kq + 4, :],
                      codes_r[:, 4 * kq:4 * kq + 4, m * 128:(m + 1) * 128],
                      f"c{w}", ckey)
            for k in range(NCH):
                p.mm(psC[:, 0:128], ind4[0:4, 0:128],
                     sc4[0:4, k * 128:(k + 1) * 128],
                     reads=("ind4t", "sc4"), writes="psC")
                p.tt(wm[:, k * 128:(k + 1) * 128], cod[:, k * 128:(k + 1) * 128],
                     psC[:, 0:128], ALU.mult, reads=(ckey, "psC"), writes=wmkey)
            if w in ("q", "k"):
                for n0 in (0, 512):
                    half = (n0 // 512) % 2
                    pw = psW[:, half * 512:half * 512 + 512]
                    pwk = "psWa" if half == 0 else "psWb"
                    p.mm(pw, brow4[0:1, m * 128:(m + 1) * 128],
                         onesrow[0:1, :], reads=("brow4", "onesrow"),
                         writes=pwk, start=True, stop=False)
                    for k in range(NCH):
                        p.mm(pw, wm[:, k * 128:(k + 1) * 128],
                             h3[:, k * TOK + n0:k * TOK + n0 + 512],
                             reads=(wmkey, "h3"), writes=pwk,
                             start=False, stop=(k == NCH - 1))
                    if w == "q":
                        p.act(qres[:, m * TOK + n0:m * TOK + n0 + 512], pw,
                              AF.Copy, reads=pwk, writes="qres", scale=ISQHD)
                    else:
                        p.act(qst[:, n0:n0 + 512], pw, AF.Copy, reads=pwk,
                              writes="qst")
                if w == "q":
                    p.act(sqv, qres[:, m * TOK:(m + 1) * TOK], AF.Square,
                          reads="qres", writes="sqv")
                else:
                    p.dma("sp", kv_o[m * 128:(m + 1) * 128, :], qst,
                          "qst", "kpay")
                    p.act(sqv, qst, AF.Square, reads="qst", writes="sqv")
                for n0 in (0, 512):
                    p.mm(psA[0:1, :], ones[:, 0:1], sqv[:, n0:n0 + 512],
                         reads=("sqv", "ones"), writes="psA")
                    p.act(nr2[0:1, n0:n0 + 512], psA[0:1, :], AF.Copy,
                          reads="psA", writes="nr2")
                p.reduce(nrow[0:1, wi * NHEAD + m:wi * NHEAD + m + 1],
                         nr2[0:1, 0:1024], ALU.max, reads="nr2", writes="nrow")
            else:
                vview = kv_o[D:2 * D, :].rearrange("(t a) b -> t (a b)", a=2)
                for tt in range(8):
                    par = tt % 2
                    pv = psC[:, 0:128] if par == 0 else psW[:, 0:128]
                    pvk = "psC" if par == 0 else "psWa"
                    p.mm(pv, onesrow[0:1, 0:128],
                         brow4[0:1, m * 128:(m + 1) * 128],
                         reads=("onesrow", "brow4"), writes=pvk,
                         start=True, stop=False)
                    for k in range(NCH):
                        p.mm(pv,
                             h3[:, k * TOK + tt * 128:k * TOK + (tt + 1) * 128],
                             wm[:, k * 128:(k + 1) * 128],
                             reads=("h3", wmkey), writes=pvk,
                             start=False, stop=(k == NCH - 1))
                    p.act(qst[:, par * 128:par * 128 + 128], pv, AF.Copy,
                          reads=pvk, writes="qst")
                    p.dma("sp", vview[tt * 128:(tt + 1) * 128,
                                      m * 128:(m + 1) * 128],
                          qst[:, par * 128:par * 128 + 128], "qst", "kpay")
    p.dma("sp", n_o, nrow[0:1, 0:32], "nrow", "npay")
    for c in range(NCH):
        p.dma("sp", q_o[c * 128:(c + 1) * 128, :],
              qres[:, c * TOK:(c + 1) * TOK], "qres", "qout")
        p.dma("sp", x2_o[c * 128:(c + 1) * 128, :],
              xres[:, c * TOK:(c + 1) * TOK], "xres", "x2out")
    p.finish()
    return nc



def build_p3():
    nc = bass.Bass("TRN2", target_bir_lowering=False)
    x_d = nc.dram_tensor("x2", [D, TOK], F32, kind="ExternalInput").ap()
    q_d = nc.dram_tensor("qin", [D, TOK], BF16, kind="ExternalInput").ap()
    kvg = nc.dram_tensor("kvg", [NCORE * 2 * D, TOK], BF16, kind="ExternalInput").ap()
    ngall = nc.dram_tensor("ngall", [1, 256], F32, kind="ExternalInput").ap()
    modr1 = nc.dram_tensor("modr1", [1, 4 * D], F32, kind="ExternalInput").ap()
    ind4_d = nc.dram_tensor("ind4", [4, 128], F32, kind="ExternalInput").ap()
    co_d = nc.dram_tensor("co", [D, D], F32, kind="ExternalInput").ap()
    so_d = nc.dram_tensor("so", [64, D], F32, kind="ExternalInput").ap()
    bo_d = nc.dram_tensor("bo", [1, D], F32, kind="ExternalInput").ap()
    wgt_d = nc.dram_tensor("wgt", [D, HID], F32, kind="ExternalInput").ap()
    wut_d = nc.dram_tensor("wut", [D, HID], F32, kind="ExternalInput").ap()
    wdt_d = nc.dram_tensor("wdt", [HID, D], F32, kind="ExternalInput").ap()
    xo_d = nc.dram_tensor("xo", [D, TOK], F32, kind="ExternalOutput").ap()
    g1_s = nc.dram_tensor("g1_s", [HID, TOK], BF16, kind="Internal").ap()

    p = Prog(nc)
    ones = p.sb("ones", [128, 1], F32)
    onesrow = p.sb("onesrow", [1, 512], F32)
    p.memset(ones[:, :], 1.0, writes="ones")
    p.memset(onesrow[0:1, :], 1.0, writes="onesrow")
    onesb = p.sb("onesb", [128, 1], BF16)
    p.memset(onesb[:, :], 1.0, writes="ones")
    PF = p.sb("PF", [128, 23040], F32)
    PB = p.sb("PB", [128, 49152], BF16)
    R = p.sb("R", [1, 4096], F32)
    bcols = p.sb("bcols", [128, 64], F32)
    ind4 = p.sb("ind4t", [4, 128], F32)
    psA = p.ps("psA", [1, 512], F32)
    psC = p.ps("psC", [128, 512], F32)
    xres = PF[:, 0:16384]
    fscr = PF[:, 16384:23040]
    p.dma("sp", ind4[0:4, :], ind4_d, "ind4d", "ind4t")
    for c in range(NCH):
        p.dma("sp", xres[:, c * TOK:(c + 1) * TOK],
              x_d[c * 128:(c + 1) * 128, :], "x_d", "xres")
    stk4 = contextlib.ExitStack()
    p.phase()
    stk5 = contextlib.ExitStack()
    psS = [p.ps("psS5a", [128, 512], F32, scope=stk5),
           p.ps("psS5b", [128, 512], F32, scope=stk5)]
    psO = [p.ps("psO5a", [128, 512], F32, scope=stk5),
           p.ps("psO5b", [128, 512], F32, scope=stk5)]
    psD = [p.ps("psD5a", [1, 512], F32, scope=stk5),
           p.ps("psD5b", [1, 512], F32, scope=stk5)]
    nrows = fscr[0:1, 0:320]
    negM = fscr[:, 512:514]
    p.dma("sp", nrows[0:1, 0:256], ngall[0:1, 0:256], "ng", "nrows")
    p.copy(nrows[0:1, 256:288], nrows[0:1, 0:32], reads="nrows", writes="nrows")
    for r in range(1, 8):
        p.tt(nrows[0:1, 256:288], nrows[0:1, 256:288],
             nrows[0:1, r * 32:r * 32 + 32], ALU.max, reads="nrows",
             writes="nrows")
    p.tt(nrows[0:1, 288:288 + 16], nrows[0:1, 256:256 + 16],
         nrows[0:1, 272:272 + 16], ALU.mult, reads="nrows", writes="nrows")
    p.act(nrows[0:1, 288:304], nrows[0:1, 288:304], AF.Ln, reads="nrows",
          writes="nrows")
    p.act(nrows[0:1, 288:304], nrows[0:1, 288:304], AF.Exp, reads="nrows",
          writes="nrows", scale=0.5)
    pT5 = PB[:, 26624:26624 + 1024]
    ot5 = fscr[:, 2048:2560]
    rec5 = fscr[0:1, 1024:1536]
    atf = PB[:, 27648:27648 + 16384]
    for hi in range(NHEAD):
        par = hi % 2
        kh5 = PB[:, 0:8192]
        vt5 = PB[:, 8192 + par * 8192:8192 + par * 8192 + 8192]
        qh5 = PB[:, 24576 + par * 1024:24576 + par * 1024 + 1024]
        khk, vtk, qhk = "kh", f"vt{par}", f"qh{par}"
        nMc = negM[:, par:par + 1]
        nMk = f"negM{par}"
        p.mm(psC[:, par:par + 1], onesrow[0:1, 0:128],
             nrows[0:1, 288 + hi:289 + hi], reads=("onesrow", "nrows"),
             writes="psC" + str(par))
        p.act(nMc, psC[:, par:par + 1], AF.Copy, reads="psC" + str(par),
              writes=nMk, scale=-1.0)
        p.dma("act", qh5, q_d[hi * 128:(hi + 1) * 128, :], "q_d", qhk)
        for r in range(NCORE):
            p.dma("sp", kh5[:, r * TOK:(r + 1) * TOK],
                  kvg[r * 2 * D + hi * 128:r * 2 * D + (hi + 1) * 128, :],
                  "kvg", khk)
        for kq in range(8):
            vsrc = kvg[kq * 2 * D + D:(kq + 1) * 2 * D, :].rearrange(
                "(k p f1) f2 -> p k (f1 f2)", k=8, p=128, f1=2)
            p.dma("act",
                  vt5[:, :].rearrange("p (k c) -> p k c", c=128)[:, 8 * kq:8 * kq + 8, :],
                  vsrc[:, :, hi * 128:(hi + 1) * 128],
                  "kvg", vtk)
        for qb in range(2):
            q0 = qb * 512
            op = qb % 2
            pO = psO[op]
            pOk = f"psO{op}"
            pD = psD[op]
            pDk = f"psD{op}"

            def st_mm(kt_i, hi=hi, q0=q0, kh5=kh5, qh5=qh5, khk=khk, qhk=qhk):
                ps = psS[kt_i % 2]
                pskey = f"psS{kt_i % 2}"
                p.mm(ps[:, :], kh5[:, kt_i * 128:(kt_i + 1) * 128],
                     qh5[:, q0:q0 + 512], reads=(khk, qhk), writes=pskey)
                pb = pT5[:, (kt_i % 2) * 512:(kt_i % 2) * 512 + 512]
                pbkey = f"pT{kt_i % 2}"
                p.op("act", lambda E, o=pb, i=ps[:, :], b=nMc:
                     E.activation(out=o, in_=i, func=AF.Exp, bias=b, scale=1.0),
                     reads=(pskey, nMk), writes=(pbkey,))

            def av_mm(kt_i, vtk=vtk, vt5=vt5, pO=pO, pOk=pOk, pD=pD, pDk=pDk):
                pb = pT5[:, (kt_i % 2) * 512:(kt_i % 2) * 512 + 512]
                pbkey = f"pT{kt_i % 2}"
                p.mm(pO[:, :], vt5[:, kt_i * 128:kt_i * 128 + 128],
                     pb[:, :], reads=(vtk, pbkey), writes=pOk,
                     start=(kt_i == 0), stop=(kt_i == 63))
                p.mm(pD[0:1, :], onesb[:, 0:1], pb[:, :],
                     reads=("ones", pbkey), writes=pDk,
                     start=(kt_i == 0), stop=(kt_i == 63))

            st_mm(0)
            for kt_i in range(1, 64):
                st_mm(kt_i)
                av_mm(kt_i - 1)
            av_mm(63)
            p.op("dve", lambda E, o=rec5[0:1, 0:512], i=pD[0:1, :]:
                 E.reciprocal(o, i), reads=(pDk,), writes=("rec5",))
            bc = psS[0]
            p.mm(bc[:, :], onesrow[0:1, 0:128], rec5[0:1, 0:512],
                 reads=("onesrow", "rec5"), writes="psS0")
            p.act(ot5[:, 0:512], bc[:, :], AF.Copy, reads="psS0", writes="ot5")
            p.tt(atf[:, hi * TOK + q0:hi * TOK + q0 + 512], pO[:, :],
                 ot5[:, 0:512], ALU.mult, reads=(pOk, "ot5"), writes="atf")
    p.phase()

    stk5.close()
    stk6 = contextlib.ExitStack()
    psB = p.ps("psB6", [1, 512], F32, scope=stk6)
    rtc = p.ps("rtc6", [128, 16], F32, scope=stk6)
    psW = p.ps("psW6", [128, 1024], F32, scope=stk6)
    psG = p.ps("psG6", [128, 512], F32, scope=stk6)
    psU = p.ps("psU6", [128, 512], F32, scope=stk6)
    grow = R[0:1, 0:2 * D]
    t1r = fscr[0:1, 2048:2048 + D]
    t2r = fscr[0:1, 4096:4096 + D]
    p.dma("sp", grow[0:1, 0:D], modr1[0:1, 0:D], "m1g", "grow6")
    p.dma("sp", grow[0:1, D:2 * D], modr1[0:1, 3 * D:4 * D], "m1g", "grow6")
    for gi in range(2):
        gseg = grow[0:1, gi * D:(gi + 1) * D]
        p.act(t1r, gseg, AF.Sign, reads="grow6", writes="t1r6")
        p.act(t2r, gseg, AF.Abs, reads="grow6", writes="t2r6")
        p.ts(t2r, t2r, 0.25, None, ALU.max, ALU.bypass, reads="t2r6",
             writes="t2r6")
        p.tt(gseg, t1r, t2r, ALU.mult, reads=("t1r6", "t2r6"), writes="grow6")
        row_to_cols(p, gseg, "grow6", bcols, "bcol6", rtc, 16, ones,
                    off=gi * 16)
    p.phase()
    brow6 = fscr[0:1, 0:D]
    p.dma("sp", brow6, bo_d, "bo", "brow6")
    codes_r = co_d.rearrange("(k p) m -> p k m", p=128)
    scales_r = so_d.rearrange("(k j) m -> j k m", j=4)
    sc4o = fscr[0:4, 2048:2048 + 2048]
    yt6 = fscr[:, 4096:5120]
    for m in range(NCH):
        p.dma("sp", sc4o[0:4, 0:2048].rearrange("j (k m) -> j k m", m=128),
              scales_r[:, :, m * 128:(m + 1) * 128], "so", "sc4o")
        cod = PB[:, 8192 + (m % 2) * 2048:8192 + (m % 2) * 2048 + 2048]
        ckey = f"co6{m % 2}"
        wm = PB[:, 12288 + (m % 2) * 2048:12288 + (m % 2) * 2048 + 2048]
        wmkey = f"wo6{m % 2}"
        for kq in range(4):
            p.dma("pool",
                  cod.rearrange("p (k m) -> p k m", m=128)[:, 4 * kq:4 * kq + 4, :],
                  codes_r[:, 4 * kq:4 * kq + 4, m * 128:(m + 1) * 128],
                  "co", ckey)
        for k in range(NCH):
            p.mm(psC[:, 0:128], ind4[0:4, 0:128],
                 sc4o[0:4, k * 128:(k + 1) * 128],
                 reads=("ind4t", "sc4o"), writes="psC")
            p.tt(wm[:, k * 128:(k + 1) * 128], cod[:, k * 128:(k + 1) * 128],
                 psC[:, 0:128], ALU.mult, reads=(ckey, "psC"), writes=wmkey)
        for n0 in (0, 512):
            half = (n0 // 512) % 2
            pw = psW[:, half * 512:half * 512 + 512]
            pwk = "psWa" if half == 0 else "psWb"
            p.mm(pw, brow6[0:1, m * 128:(m + 1) * 128],
                 onesrow[0:1, :], reads=("brow6", "onesrow"), writes=pwk,
                 start=True, stop=False)
            for k in range(NCH):
                p.mm(pw, wm[:, k * 128:(k + 1) * 128],
                     atf[:, k * TOK + n0:k * TOK + n0 + 512],
                     reads=(wmkey, "atf"), writes=pwk,
                     start=False, stop=(k == NCH - 1))
            p.act(yt6[:, n0:n0 + 512], pw, AF.Copy, reads=pwk, writes="yt6")
        p.stt(xres[:, m * TOK:(m + 1) * TOK], yt6, bcols[:, m:m + 1],
              xres[:, m * TOK:(m + 1) * TOK], ALU.mult, ALU.add,
              reads=("yt6", "bcol6", "xres"), writes="xres")
    p.phase()
    p.dma("sp", R[0:1, 0:2 * D], modr1[0:1, D:3 * D], "m1g", "Rln4")
    h4 = PB[:, 0:16384]
    emit_ln_affine(p, xres, "xres", h4, "h4", R, "Rln4", 0, ones, "f4",
                  TOK, psA, psB, psC, fscr, onesrow=onesrow)
    p.phase()
    wgr1 = wgt_d.rearrange("(k p) h -> p k h", p=128)
    wur1 = wut_d.rearrange("(k p) h -> p k h", p=128)
    sg6 = fscr[:, 0:1024]
    su6 = fscr[:, 1024:2048]
    ght = PB[:, 24576:24576 + 1024]
    for hc in range(64):
        wgt6 = PB[:, 16384 + (hc % 2) * 2048:16384 + (hc % 2) * 2048 + 2048]
        wut6 = PB[:, 20480 + (hc % 2) * 2048:20480 + (hc % 2) * 2048 + 2048]
        gkey, ukey = f"wgt6{hc % 2}", f"wut6{hc % 2}"
        for kq in range(4):
            p.dma("pool",
                  wgt6.rearrange("p (k h) -> p k h", h=128)[:, 4 * kq:4 * kq + 4, :],
                  wgr1[:, 4 * kq:4 * kq + 4, hc * 128:(hc + 1) * 128],
                  "wgt_d", gkey)
            p.dma("pool",
                  wut6.rearrange("p (k h) -> p k h", h=128)[:, 4 * kq:4 * kq + 4, :],
                  wur1[:, 4 * kq:4 * kq + 4, hc * 128:(hc + 1) * 128],
                  "wut_d", ukey)
        for n0 in (0, 512):
            par6 = (hc * 2 + n0 // 512) % 2
            pg = psG[:, :] if par6 == 0 else psW[:, 0:512]
            pu = psU[:, :] if par6 == 0 else psW[:, 512:1024]
            pgk = "psG" if par6 == 0 else "psWa"
            puk = "psU" if par6 == 0 else "psWb"
            for k in range(NCH):
                p.mm(pg, wgt6[:, k * 128:(k + 1) * 128],
                     h4[:, k * TOK + n0:k * TOK + n0 + 512],
                     reads=(gkey, "h4"), writes=pgk,
                     start=(k == 0), stop=(k == NCH - 1))
            for k in range(NCH):
                p.mm(pu, wut6[:, k * 128:(k + 1) * 128],
                     h4[:, k * TOK + n0:k * TOK + n0 + 512],
                     reads=(ukey, "h4"), writes=puk,
                     start=(k == 0), stop=(k == NCH - 1))
            p.act(sg6[:, n0:n0 + 512], pg, AF.Silu, reads=pgk,
                  writes="sg6")
            p.act(su6[:, n0:n0 + 512], pu, AF.Copy, reads=puk,
                  writes="su6")
        p.tt(ght, sg6, su6, ALU.mult, reads=("sg6", "su6"), writes="ght")
        p.dma("sp", g1_s[hc * 128:(hc + 1) * 128, :], ght, "ght", "g1_s")
    p.phase()
    wdr1 = wdt_d.rearrange("(k p) d -> p k d", p=128)
    for th_ in range(2):
        gch = PB[:, 0:64 * 512]
        for k in range(64):
            p.dma("sp", gch[:, k * 512:(k + 1) * 512],
                  g1_s[k * 128:(k + 1) * 128, th_ * 512:(th_ + 1) * 512],
                  "g1_s", "gch6")
        for d in range(NCH):
            db = d % 2
            wdb = PB[:, 64 * 512 + db * HID:64 * 512 + db * HID + HID]
            wk6 = f"wdb6{db}"
            for kq in range(8):
                p.dma("pool",
                      wdb.rearrange("p (k m) -> p k m", m=128)[:, 8 * kq:8 * kq + 8, :],
                      wdr1[:, 8 * kq:8 * kq + 8, d * 128:(d + 1) * 128],
                      "wdt_d", wk6)
            pw = psW[:, db * 512:db * 512 + 512]
            pwk = "psWa" if db == 0 else "psWb"
            for k in range(64):
                p.mm(pw, wdb[:, k * 128:(k + 1) * 128],
                     gch[:, k * 512:(k + 1) * 512], reads=(wk6, "gch6"),
                     writes=pwk, start=(k == 0), stop=(k == 63))
            p.stt(xres[:, d * TOK + th_ * 512:d * TOK + th_ * 512 + 512],
                  pw, bcols[:, 16 + d:17 + d],
                  xres[:, d * TOK + th_ * 512:d * TOK + th_ * 512 + 512],
                  ALU.mult, ALU.add, reads=(pwk, "bcol6", "xres"),
                  writes="xres")
    for c in range(NCH):
        p.dma("sp", xo_d[c * 128:(c + 1) * 128, :],
              xres[:, c * TOK:(c + 1) * TOK], "xres", "xo")
    p.finish()
    return nc




def kernel_3l(**inp):
    f32 = np.float32
    CORES = list(range(NCORE))
    MC0B = 4 * D // NCORE
    MC1 = 6 * D // NCORE
    x_fm = _cc(inp["L0_attn_x"][0].astype(f32).T)
    wgx2 = _cc(inp["L0_attn_Wg_x"].reshape(D, 128))
    wga2 = _cc(inp["L0_attn_Wg_a"].reshape(D, 128))
    ind4 = np.zeros((4, 128), f32)
    for j in range(4):
        ind4[j, j * 32:(j + 1) * 32] = 1.0
    wm0a = _cc(inp["L0_wmod"][:, 0:2 * D])

    nc = build_p1()
    im = []
    for r in range(NCORE):
        ts_ = slice(r * TOK, (r + 1) * TOK)
        im.append({
            "x": _cc(x_fm[:, ts_]), "c0": inp["L0_c"], "c1": inp["L1_c"],
            "wm0a": wm0a,
            "wm0b": _cc(inp["L0_wmod"][:, 2 * D + r * MC0B:2 * D + (r + 1) * MC0B]),
            "wm1": _cc(inp["L1_wmod"][:, r * MC1:(r + 1) * MC1]),
            "win": inp["L0_attn_W_in"], "binr": inp["L0_attn_b_in"][None, :],
            "wgx": wgx2, "wga": wga2,
            "bgx": inp["L0_attn_bg_x"][None, :],
            "bga": inp["L0_attn_bg_a"][None, :],
            "ar": inp["L0_attn_a_real_param"][None, :],
            "ai": inp["L0_attn_a_imag_param"][None, :]})
    res = run_bass_kernel_spmd(nc, im, core_ids=CORES)
    r1 = res.results
    carg = np.concatenate([r1[r]["summ_o"] for r in range(NCORE)], axis=0)
    modr0 = np.concatenate([r1[r]["modx_o"][:, 0:MC0B] for r in range(NCORE)],
                           axis=1)
    mod1 = np.concatenate([r1[r]["modx_o"][:, MC0B:MC0B + MC1]
                           for r in range(NCORE)], axis=1)
    mods1 = _cc(mod1[:, 0:2 * D])
    modr1 = _cc(mod1[:, 2 * D:6 * D])

    nc = build_p2()
    im = []
    for r in range(NCORE):
        ts_ = slice(r * TOK, (r + 1) * TOK)
        alphac = np.zeros((128, 7), f32)
        alphac[:, 0:r] = 1.0
        im.append({
            "x": _cc(x_fm[:, ts_]), "carg": carg, "alphac": alphac,
            "ind4": ind4,
            "st_ur": r1[r]["st_ur"], "st_ui": r1[r]["st_ui"],
            "st_ec": r1[r]["st_ec"], "st_cs": r1[r]["st_cs"],
            "st_sn": r1[r]["st_sn"],
            "modr0": modr0, "mods1": mods1,
            "wout": inp["L0_attn_W_out"], "boutr": inp["L0_attn_b_out"][None, :],
            "wg": inp["L0_ffn_w_gate"], "wu": inp["L0_ffn_w_up"],
            "wd": inp["L0_ffn_w_down"],
            "cq": inp["L1_attn_wq_codes"], "sq": inp["L1_attn_wq_scales"],
            "bq": inp["L1_attn_bq"][None, :],
            "ck": inp["L1_attn_wk_codes"], "sk": inp["L1_attn_wk_scales"],
            "bk": inp["L1_attn_bk"][None, :],
            "cv": inp["L1_attn_wv_codes"], "sv": inp["L1_attn_wv_scales"],
            "bv": inp["L1_attn_bv"][None, :]})
    res = run_bass_kernel_spmd(nc, im, core_ids=CORES)
    r2 = res.results
    kvg = np.concatenate([np.asarray(r2[r]["kv_o"]).view(BF)
                          for r in range(NCORE)], axis=0)
    ngall = np.concatenate([r2[r]["n_o"] for r in range(NCORE)], axis=1)

    nc = build_p3()
    wgt = _cc(inp["L1_ffn_w_gate"].T)
    wut = _cc(inp["L1_ffn_w_up"].T)
    wdt = _cc(inp["L1_ffn_w_down"].T)
    im = []
    for r in range(NCORE):
        im.append({
            "x2": r2[r]["x2_o"], "qin": np.asarray(r2[r]["q_o"]).view(BF),
            "kvg": kvg, "ngall": ngall, "modr1": modr1, "ind4": ind4,
            "co": inp["L1_attn_wo_codes"], "so": inp["L1_attn_wo_scales"],
            "bo": inp["L1_attn_bo"][None, :],
            "wgt": wgt, "wut": wut, "wdt": wdt})
    res = run_bass_kernel_spmd(nc, im, core_ids=CORES)
    xo = np.concatenate([res.results[r]["xo"] for r in range(NCORE)], axis=1)
    return _cc(xo.T)[None].astype(np.float32)
